# version 12
# baseline (speedup 1.0000x reference)
import math
from contextlib import ExitStack

import numpy as np
import concourse.bass as bass
import concourse.mybir as mybir
from concourse.alu_op_type import AluOpType
from concourse.bass_utils import run_bass_kernel_spmd

F32 = mybir.dt.float32
BF16 = mybir.dt.bfloat16
AF = mybir.ActivationFunctionType
AX = mybir.AxisListType
OP = AluOpType

NEG = -1.0e10
EPS = 1.0e-5


class Cfg:
    def __init__(self, S=2048, D=2048, H=16, E=8, F=1408, dump=None, upto="p4"):
        self.S, self.D, self.H, self.E, self.F = S, D, H, E, F
        self.upto = upto
        self.DH = 128
        self.FS = 2 * F
        self.DC = D // 128
        self.TB = S // 128
        self.TB4 = S // 512
        self.QG = self.TB // 2
        self.FC = F // 128
        self.HB = D // 512
        self.HB2 = D // 256
        self.NEX = E + 2
        self.dump = dump
        assert D == H * 128 and S % 512 == 0 and D % 512 == 0 and F % 128 == 0


class FakeInstr:
    def then_inc(self, *_a, **_k):
        return self


class FakeEng:
    def __getattr__(self, name):
        def f(*_a, **_k):
            return FakeInstr()
        return f


class P:

    def __init__(self, nc, stack):
        self.nc = nc
        self.sem = {}
        for n in ("dW", "dX", "dO", "tP", "tV", "tS", "tG"):
            self.sem[n] = stack.enter_context(nc.semaphore(n))
        self.reset(dry=True)
        self.marks = {}

    def reset(self, dry):
        self.dry = dry
        self.cnt = {n: 0 for n in ("dW", "dX", "dO", "tP", "tV", "tS", "tG")}
        self.seen = {}

    def inc(self, instr, s, n=1):
        self.cnt[s] += n
        if not self.dry:
            instr.then_inc(self.sem[s], n)
        return self.cnt[s]

    def mark(self, name, s):
        if self.dry:
            self.marks[name] = (s, self.cnt[s])

    def wait(self, eng, engid, name):
        if self.dry:
            return
        s, v = self.marks[name]
        if v <= 0:
            return
        key = (engid, s)
        if self.seen.get(key, -1) >= v:
            return
        eng.wait_ge(self.sem[s], v)
        self.seen[key] = v

    def wait_cnt(self, eng, engid, s, v):
        if self.dry or v <= 0:
            return
        key = (engid, s)
        if self.seen.get(key, -1) >= v:
            return
        eng.wait_ge(self.sem[s], v)
        self.seen[key] = v


def emit(p, cfg, eng, io, buf):
    S, D, H, E, F = cfg.S, cfg.D, cfg.H, cfg.E, cfg.F
    DC, TB, TB4, QG, FC, HB, HB2, NEX = (cfg.DC, cfg.TB, cfg.TB4, cfg.QG,
                                         cfg.FC, cfg.HB, cfg.HB2, cfg.NEX)
    sy, g, t, v, s = eng["sync"], eng["gpsimd"], eng["tensor"], eng["vector"], eng["scalar"]
    (x_t, x1_t, out_t, wqkv, wo, gatew, gu_w, dn_w, dump_d) = io
    (fm, zy, wk, wg, comb, gw_sb, ones, id_f, id_b, mask_b,
     bnb, mvb, tln, rstd, negmr, lsum, rcol, nmax, rrow, s1b,
     pr8, s2b, mk1, mk2, l8, r8, m8, epsb,
     pb0, pb1, pb2, pb3, pb4) = buf

    wkb = wk[:].bitcast(F32)
    wgb = wg[:].bitcast(F32)
    x0b = [wkb[:, i * D:(i + 1) * D] for i in range(2)]
    araw = wkb[:, 2 * D:3 * D]
    t1b = wkb[:, 3 * D:4 * D]
    lnt = [wk[:, 8 * D + i * D: 8 * D + (i + 1) * D] for i in range(2)]
    qh = wk[:, 0:S]
    kh = wk[:, S:2 * S]
    vtm = wk[:, 2 * S:3 * S]
    prb = [wk[:, 3 * S + i * S: 3 * S + (i + 1) * S] for i in range(2)]
    prT = wk[:, 5 * S:7 * S]
    vfm = [wk[:, 7 * S + i * 512: 7 * S + (i + 1) * 512] for i in range(2)]
    bc = wkb[:, (7 * S + 1024) // 2: (7 * S + 1024) // 2 + 256]
    scb = [wk[:, 7 * S + 1792 + i * S: 7 * S + 1792 + (i + 1) * S] for i in range(2)]
    act = wk
    wqb = [wg[:, i * DC * 384:(i + 1) * DC * 384] for i in range(2)]
    x1b = [wgb[:, i * D:(i + 1) * D] for i in range(2)]
    ln2f = wgb[:, 2 * D:3 * D]
    rot = [wgb[:, 3 * D + i * 512: 3 * D + (i + 1) * 512] for i in range(2)]
    gub = [wg[:, i * DC * 256:(i + 1) * DC * 256] for i in range(2)]
    dnb = [wg[:, 2 * DC * 256 + i * FC * 256: 2 * DC * 256 + (i + 1) * FC * 256]
           for i in range(2)]
    pb1h = pb1[:].bitcast(BF16)
    pb2h = pb2[:].bitcast(BF16)

    DH = 128
    dump = cfg.dump
    H_RUN = 1 if cfg.upto == "p1h0" else H

    ci = sy.dma_start(out=gw_sb[:].rearrange("p (c n) -> p c n", n=E),
                      in_=gatew.rearrange("c p n -> p c n"))
    p.inc(ci, "dX", 16)
    p.mark("gw", "dX")

    ci = g.memset(ones[:], 1.0)
    p.inc(ci, "tG")
    ci = g.memset(mask_b[:], 0.0)
    p.inc(ci, "tG")
    ci = g.memset(epsb[:], EPS)
    p.inc(ci, "tG")
    p.wait_cnt(g, "gpsimd", "tG", p.cnt["tG"] if not p.dry else 0) if False else None
    c0 = p.cnt["tG"]
    if not p.dry:
        g.wait_ge(p.sem["tG"], c0)
    ci = g.affine_select(id_f[:], ones[:], [[-1, 128]], OP.is_equal, 0.0,
                        base=0, channel_multiplier=1)
    p.inc(ci, "tG")
    c1 = p.cnt["tG"]
    if not p.dry:
        g.wait_ge(p.sem["tG"], c1)
    ci = g.tensor_copy(id_b[:], id_f[:])
    p.inc(ci, "tG")
    ci = g.affine_select(mask_b[:], mask_b[:], [[-1, 128]], OP.is_ge, NEG,
                         base=0, channel_multiplier=1)
    p.inc(ci, "tG")
    c2 = p.cnt["tG"]
    if not p.dry:
        g.wait_ge(p.sem["tG"], c2)
    p.mark("const", "tG")

    TRG = 8
    n_tr_groups = (DC + TRG - 1) // TRG

    x0q = [x0b[0], x0b[1], araw, t1b]
    for tb in range(TB):
        par = tb % 2
        if tb >= 4:
            p.wait(sy, "sync", f"p0_ln_{tb-4}")
        ci = sy.dma_start(out=x0q[tb % 4], in_=x_t[tb, :, :])
        p.inc(ci, "dX", 16)
        p.mark(f"p0_x0_{tb}", "dX")
        p.wait(v, "vector", f"p0_x0_{tb}")
        if tb >= 2:
            p.wait(v, "vector", f"p0_rstd_{tb-2}")
        for j in range(D // 512):
            v.bn_stats(bnb[:, par * 32 + 6 * j: par * 32 + 6 * j + 6],
                       x0q[tb % 4][:, j * 512:(j + 1) * 512])
        v.drain()
        ci = v.bn_aggr(mvb[:, par * 2: par * 2 + 2],
                       bnb[:, par * 32: par * 32 + 6 * (D // 512)])
        p.inc(ci, "tV")
        p.mark(f"p0_mv_{tb}", "tV")
        p.wait(s, "scalar", f"p0_mv_{tb}")
        p.wait(s, "scalar", "const")
        s.activation(tln[:, par: par + 1], mvb[:, par * 2 + 1: par * 2 + 2],
                     AF.Ln, bias=epsb[:, 0:1], scale=1.0)
        s.drain()
        ci = s.activation(rstd[:, par: par + 1], tln[:, par: par + 1],
                          AF.Exp, bias=0.0, scale=-0.5)
        p.inc(ci, "tS")
        p.mark(f"p0_rstd_{tb}", "tS")
        s.drain()
        p.wait(v, "vector", f"p0_rstd_{tb}")
        ci = v.tensor_scalar(negmr[:, par: par + 1], mvb[:, par * 2: par * 2 + 1],
                             rstd[:, par: par + 1], -1.0, OP.mult, OP.mult)
        p.inc(ci, "tV")
        p.mark(f"p0_negmr_{tb}", "tV")
        p.wait(s, "scalar", f"p0_negmr_{tb}")
        if tb >= 2:
            p.wait(s, "scalar", f"p0_tr_{tb-2}_{n_tr_groups-1}")
        ci = s.activation(lnt[par], x0q[tb % 4], AF.Identity,
                          bias=negmr[:, par: par + 1], scale=rstd[:, par: par + 1])
        p.inc(ci, "tS")
        p.mark(f"p0_ln_{tb}", "tS")
        p.wait(t, "tensor", "const")
        p.wait(t, "tensor", f"p0_ln_{tb}")
        for gi in range(n_tr_groups):
            ctr = tb * n_tr_groups + gi
            bpar = ctr % 2
            ptr = pb1h if bpar == 0 else pb2h
            nt = min(TRG, DC - gi * TRG)
            if ctr >= 2:
                pc = ctr - 2
                p.wait(t, "tensor", f"p0_ev_{pc // n_tr_groups}_{pc % n_tr_groups}")
            ci = None
            for j in range(nt):
                dc = gi * TRG + j
                ci = t.transpose(ptr[:, j * 128:(j + 1) * 128],
                                 lnt[par][:, dc * 128:(dc + 1) * 128], id_b[:])
            p.inc(ci, "tP")
            p.mark(f"p0_tr_{tb}_{gi}", "tP")
            p.wait(v, "vector", f"p0_tr_{tb}_{gi}")
            src = ptr[:, 0:nt * 128].rearrange("p (c n) -> p c n", n=128)
            dst = bass.AP(fm, gi * TRG * S + tb * 128,
                          [[fm.shape[1], 128], [S, nt], [1, 128]])
            ci = v.tensor_copy(dst, src)
            p.inc(ci, "tV")
            p.mark(f"p0_ev_{tb}_{gi}", "tV")
    p.mark("ln1_done", "tV")

    if dump == "ln1fm":
        p.wait(g, "gpsimd", "ln1_done")
        ci = g.dma_start(out=dump_d.rearrange("(c p) n -> p c n", p=128),
                          in_=fm[:, 0:DC * S].rearrange("p (c n) -> p c n", n=S))
        p.inc(ci, "dO", 16)
    if cfg.upto == "p0":
        if not p.dry:
            sy.wait_ge(p.sem["dO"], p.cnt["dO"])
        return
    for h in range(H_RUN):
        if h < 2:
            p.wait(g, "gpsimd", f"p0_tr_{TB-1}_{n_tr_groups-1}")
        else:
            p.wait(g, "gpsimd", f"p1_qkvmm_{h-2}")
        ci = g.dma_start(out=wqb[h % 2].rearrange("p (c n) -> p c n", n=384),
                         in_=wqkv[h].rearrange("c p n -> p c n"))
        p.inc(ci, "dW", 16)
        p.mark(f"p1_w_{h}", "dW")

    trpar = [0]
    last_tr_ev = [None, None]

    def tr_group(eng_t, srcs, dst_builder, name):
        par = trpar[0] % 2
        bank = pb1 if par == 0 else pb2
        bankh = pb1h if par == 0 else pb2h
        if last_tr_ev[par] is not None:
            p.wait(t, "tensor", last_tr_ev[par])
        ci = None
        for j, src in enumerate(srcs):
            ci = t.transpose(bankh[:, j * 128:(j + 1) * 128], src, id_b[:])
        p.inc(ci, "tP")
        p.mark(f"{name}_tr", "tP")
        p.wait(v, "vector", f"{name}_tr")
        src = bankh[:, 0:len(srcs) * 128].rearrange("p (c n) -> p c n", n=128)
        ci = v.tensor_copy(dst_builder(len(srcs)), src)
        p.inc(ci, "tV")
        p.mark(f"{name}_ev", "tV")
        last_tr_ev[par] = f"{name}_ev"
        trpar[0] += 1

    for h in range(H_RUN):
        p.wait(t, "tensor", "ln1_done")
        p.wait(t, "tensor", f"p1_w_{h}")
        for tb in range(TB4):
            if tb > 0:
                p.wait(t, "tensor", f"p1_q_{h}_{tb-1}")
                p.wait(t, "tensor", f"p1_kv_{h}_{tb-1}")
            elif h > 0:
                p.wait(t, "tensor", f"p1_sev_{h-1}_{TB-1}")
            for which in range(3):
                dst = pb0[:, which * 512:(which + 1) * 512]
                ci = None
                for dc in range(DC):
                    ci = t.matmul(dst,
                                  wqb[h % 2][:, dc * 384 + which * DH: dc * 384 + (which + 1) * DH],
                                  fm[:, dc * S + tb * 512: dc * S + tb * 512 + 512],
                                  start=(dc == 0), stop=(dc == DC - 1))
                p.inc(ci, "tP")
                p.mark(f"p1_qkv_{h}_{tb}_{which}", "tP")
            p.wait(s, "scalar", f"p1_qkv_{h}_{tb}_0")
            if h > 0 and tb == 0:
                p.wait(s, "scalar", f"p1_sc_{h-1}_{TB-1}")
            ci = s.activation(qh[:, tb * 512:(tb + 1) * 512],
                              pb0[:, 0:512], AF.Copy, bias=0.0,
                              scale=1.0 / math.sqrt(DH))
            p.inc(ci, "tS")
            p.mark(f"p1_q_{h}_{tb}", "tS")
            p.wait(v, "vector", f"p1_qkv_{h}_{tb}_1")
            p.wait(v, "vector", f"p1_qkv_{h}_{tb}_2")
            if h > 0 and tb == 0:
                p.wait(v, "vector", f"p1_sc_{h-1}_{TB-1}")
            if tb >= 2:
                p.wait(v, "vector", f"p1_vtr_{h}_{tb-2}_tr")
            elif h > 0:
                p.wait(v, "vector", f"p1_vtr_{h-1}_{TB4-2}_tr" if TB4 >= 2 else f"p1_vtr_{h-1}_{TB4-1}_tr")
            v.tensor_copy(kh[:, tb * 512:(tb + 1) * 512], pb0[:, 512:1024])
            ci = v.tensor_copy(vfm[tb % 2], pb0[:, 1024:1536])
            p.inc(ci, "tV")
            p.mark(f"p1_kv_{h}_{tb}", "tV")
            if h > 0 and tb == 0:
                p.wait(t, "tensor", f"p1_av_{h-1}_{QG-1}")
            p.wait(t, "tensor", f"p1_kv_{h}_{tb}")
            tr_group(t, [vfm[tb % 2][:, j * 128:(j + 1) * 128] for j in range(4)],
                     lambda n, tb=tb: bass.AP(wk, 2 * S + tb * 512,
                                              [[wk.shape[1], 128], [128, n], [1, 128]]),
                     f"p1_vtr_{h}_{tb}")
        p.mark(f"p1_qkvmm_{h}", "tP")

        def attn_scores(qt):
            par2 = qt % 2
            W_ = (qt + 1) * 128
            if qt < 8:
                off = par2 * 1024
                if qt >= 2:
                    p.wait(t, "tensor", f"p1_sev_{h}_{qt-2}")
                elif h > 0:
                    p.wait(t, "tensor", f"p1_sev_{h-1}_{TB-1}")
            else:
                off = 0
                p.wait(t, "tensor", f"p1_sev_{h}_{qt-1}")
            p.wait(t, "tensor", f"p1_q_{h}_{min(qt // 4, TB4 - 1)}")
            p.wait(t, "tensor", f"p1_kv_{h}_{min(qt // 4, TB4 - 1)}")
            if qt == 0:
                p.wait(t, "tensor", f"p1_q_{h}_{TB4-1}")
                p.wait(t, "tensor", f"p1_kv_{h}_{TB4-1}")
            nfull = (qt * 128) // 512
            for g5 in range(nfull):
                t.matmul(pb0[:, off + g5 * 512: off + (g5 + 1) * 512],
                         qh[:, qt * 128:(qt + 1) * 128],
                         kh[:, g5 * 512:(g5 + 1) * 512], start=True, stop=True)
            rem = qt * 128 - nfull * 512
            if rem > 0:
                t.matmul(pb0[:, off + nfull * 512: off + qt * 128],
                         qh[:, qt * 128:(qt + 1) * 128],
                         kh[:, nfull * 512: qt * 128], start=True, stop=True)
            t.matmul(pb0[:, off + qt * 128: off + (qt + 1) * 128],
                     qh[:, qt * 128:(qt + 1) * 128],
                     kh[:, qt * 128:(qt + 1) * 128], start=True, stop=False)
            ci = t.matmul(pb0[:, off + qt * 128: off + (qt + 1) * 128],
                          id_b[:], mask_b[:], start=False, stop=True)
            p.inc(ci, "tP")
            p.mark(f"p1_sc_{h}_{qt}", "tP")
            p.wait(v, "vector", f"p1_sc_{h}_{qt}")
            if qt >= 2:
                p.wait(v, "vector", f"p1_exp_{h}_{qt-2}")
            ci = v.tensor_copy(scb[par2][:, 0:W_], pb0[:, off: off + W_])
            p.inc(ci, "tV")
            p.mark(f"p1_sev_{h}_{qt}", "tV")
            v.drain()
            ci = v.reduce_max(nmax[:, par2: par2 + 1], scb[par2][:, 0:W_],
                              axis=AX.X, negate=True)
            p.inc(ci, "tV")
            p.mark(f"p1_max_{h}_{qt}", "tV")
            p.wait(s, "scalar", f"p1_max_{h}_{qt}")
            if qt >= 2:
                p.wait(s, "scalar", f"p1_ptr_{h}_{qt-2}_{(qt-2)//TRG}_tr")
            elif h > 0:
                lq = TB - 2 + par2
                p.wait(s, "scalar", f"p1_ptr_{h-1}_{lq}_{lq//TRG}_tr")
            ci = s.activation(prb[par2][:, 0:W_],
                              scb[par2][:, 0:W_], AF.Exp,
                              bias=nmax[:, par2: par2 + 1], scale=1.0,
                              accum_out=lsum[:, par2: par2 + 1])
            p.inc(ci, "tS")
            p.mark(f"p1_exp_{h}_{qt}", "tS")

        def attn_post(qt):
            par2 = qt % 2
            qg = qt // 2
            p.wait(v, "vector", f"p1_exp_{h}_{qt}")
            if qt >= 2:
                p.wait(v, "vector", f"p1_rtr_{h}_{qt-2}")
            ci = v.reciprocal(rcol[:, par2: par2 + 1], lsum[:, par2: par2 + 1])
            p.inc(ci, "tV")
            p.mark(f"p1_rcol_{h}_{qt}", "tV")
            p.wait(t, "tensor", f"p1_exp_{h}_{qt}")
            if par2 == 0:
                if qg > 0:
                    p.wait(v, "vector", f"p1_av_{h}_{qg-1}")
                elif h > 0:
                    p.wait(v, "vector", f"p1_av_{h-1}_{QG-1}")
                ci = v.memset(prT[:, (2 * qg + 1) * 256:(2 * qg + 1) * 256 + 128], 0.0)
                p.inc(ci, "tV")
                p.mark(f"p1_prz_{h}_{qg}", "tV")
            nkt = qt + 1
            for gi in range((nkt + TRG - 1) // TRG):
                nt = min(TRG, nkt - gi * TRG)
                if par2 == 0 and gi == 0 and qg > 0:
                    p.wait(t, "tensor", f"p1_av_{h}_{qg-1}")
                srcs = [prb[par2][:, (gi * TRG + j) * 128:(gi * TRG + j + 1) * 128]
                        for j in range(nt)]
                tr_group(t, srcs,
                         lambda n, gi=gi, par2=par2: bass.AP(
                             wk, 5 * S + gi * TRG * 256 + par2 * 128,
                             [[wk.shape[1], 128], [256, n], [1, 128]]),
                         f"p1_ptr_{h}_{qt}_{gi}")
            p.wait(t, "tensor", f"p1_rcol_{h}_{qt}")
            if par2 == 0:
                if qg > 0:
                    p.wait(t, "tensor", f"p1_bcev_{h}_{qg-1}")
                elif h > 0:
                    p.wait(t, "tensor", f"p1_bcev_{h-1}_{QG-1}")
            ci = t.transpose(pb4[0:1, par2 * 128:(par2 + 1) * 128],
                             rcol[:, par2: par2 + 1], id_f[:])
            p.inc(ci, "tP")
            p.mark(f"p1_rtr_{h}_{qt}", "tP")
            if par2 == 0:
                return
            p.wait(s, "scalar", f"p1_rtr_{h}_{2*qg+1}")
            if qg > 0:
                p.wait(s, "scalar", f"p1_bc_{h}_{qg-1}")
            elif h > 0:
                p.wait(s, "scalar", f"p1_bc_{h-1}_{QG-1}")
            ci = s.copy(rrow[0:1, :], pb4[0:1, 0:256])
            p.inc(ci, "tS")
            p.mark(f"p1_rrowev_{h}_{qg}", "tS")
            p.wait(t, "tensor", f"p1_rrowev_{h}_{qg}")
            if qg > 0:
                p.wait(t, "tensor", f"p1_bcev_{h}_{qg-1}")
            elif h > 0:
                p.wait(t, "tensor", f"p1_bcev_{h-1}_{QG-1}")
            ci = t.matmul(pb4[:, 256:512], ones[0:1, :], rrow[0:1, :],
                          start=True, stop=True)
            p.inc(ci, "tP")
            p.mark(f"p1_bc_{h}_{qg}", "tP")
            p.wait(v, "vector", f"p1_bc_{h}_{qg}")
            ci = v.tensor_copy(bc, pb4[:, 256:512])
            p.inc(ci, "tV")
            p.mark(f"p1_bcev_{h}_{qg}", "tV")
            nkt2 = 2 * qg + 2
            p.wait(t, "tensor", f"p1_ptr_{h}_{2*qg+1}_{(nkt2-1)//TRG}_ev")
            p.wait(t, "tensor", f"p1_prz_{h}_{qg}")
            if qg > 0:
                p.wait(t, "tensor", f"p1_z_{h}_{qg-1}")
            elif h > 0:
                p.wait(t, "tensor", f"p1_z_{h-1}_{QG-1}")
            ci = None
            for kt in range(nkt2):
                ci = t.matmul(pb3[:, 0:256], vtm[:, kt * 128:(kt + 1) * 128],
                              prT[:, kt * 256:(kt + 1) * 256],
                              start=(kt == 0), stop=(kt == nkt2 - 1))
            p.inc(ci, "tP")
            p.mark(f"p1_av_{h}_{qg}", "tP")
            p.wait(v, "vector", f"p1_av_{h}_{qg}")
            ci = v.tensor_tensor(zy[:, h * S + qg * 256: h * S + (qg + 1) * 256],
                                 pb3[:, 0:256], bc, OP.mult)
            p.inc(ci, "tV")
            p.mark(f"p1_z_{h}_{qg}", "tV")

        for qt in range(TB):
            attn_scores(qt)
            if qt >= 1:
                attn_post(qt - 1)
        attn_post(TB - 1)
    p.mark("attn_done", "tV")
    if dump == "prtend":
        p.wait(g, "gpsimd", "attn_done")
        ci = g.dma_start(out=dump_d, in_=prT)
        p.inc(ci, "dO", 16)
    if dump == "bcend":
        p.wait(g, "gpsimd", "attn_done")
        ci = g.dma_start(out=dump_d, in_=bc)
        p.inc(ci, "dO", 16)
    if dump == "zth0":
        p.wait(g, "gpsimd", f"p1_z_0_{QG-1}")
        ci = g.dma_start(out=dump_d, in_=zy[:, 0:S])
        p.inc(ci, "dO", 16)
    if dump == "qkvh0":
        p.wait(g, "gpsimd", "p1_q_0_" + str(TB4 - 1))
        p.wait(g, "gpsimd", "p1_kv_0_" + str(TB4 - 1))
        p.wait(g, "gpsimd", f"p1_vtr_0_{TB4-1}_ev")
        for idx, srcb in enumerate([qh, kh, vtm]):
            ci = g.dma_start(out=dump_d.rearrange("(c p) n -> p c n", p=128)[:, idx:idx+1, :],
                             in_=srcb.unsqueeze(1) if hasattr(srcb, 'unsqueeze') else srcb)
            p.inc(ci, "dO", 16)
    if dump == "prh0":
        p.wait(g, "gpsimd", "p1_exp_0_0")
        p.wait(g, "gpsimd", "p1_exp_0_1")
        for idx, srcb in enumerate(prb):
            ci = g.dma_start(out=dump_d.rearrange("(c p) n -> p c n", p=128)[:, idx:idx+1, :],
                             in_=srcb.unsqueeze(1) if hasattr(srcb, 'unsqueeze') else srcb)
            p.inc(ci, "dO", 16)

    if dump == "zt":
        p.wait(g, "gpsimd", "attn_done")
        ci = g.dma_start(out=dump_d.rearrange("(c p) n -> p c n", p=128),
                          in_=zy[:, 0:H * S].rearrange("p (c n) -> p c n", n=S))
        p.inc(ci, "dO", 16)

    if cfg.upto in ("p1", "p1h0"):
        if not p.dry:
            sy.wait_ge(p.sem["dO"], p.cnt["dO"])
        return
    for hc in range(H):
        p.wait(g, "gpsimd", f"p1_qkvmm_{H-1}")
        ci = g.dma_start(out=fm[:, hc * D:(hc + 1) * D], in_=wo[hc])
        p.inc(ci, "dW", 16)
    p.mark("p2_wo", "dW")

    if cfg.upto == "p2w":
        if dump == "wofm":
            p.wait(g, "gpsimd", "p2_wo")
            ci = g.dma_start(out=dump_d.rearrange("(c p) n -> p c n", p=128),
                             in_=fm[:, 0:H * D].rearrange("p (c n) -> p c n", n=D))
            p.inc(ci, "dO", 16)
        if not p.dry:
            sy.wait_ge(p.sem["dO"], p.cnt["dO"])
        return

    def p2_load_x0(tb):
        if tb >= TB:
            return
        if tb < 2:
            p.wait(sy, "sync", f"p0_ln_{TB-1}")
            p.wait(sy, "sync", f"p1_z_{H-1}_{QG-1}")
        else:
            p.wait(sy, "sync", f"p2_x1st_{tb-2}")
        ci = sy.dma_start(out=x0b[tb % 2], in_=x_t[tb, :, :])
        p.inc(ci, "dX", 16)
        p.mark(f"p2_x0_{tb}", "dX")

    if cfg.upto not in ("p2m", "p2t"):
        p2_load_x0(0)
        p2_load_x0(1)

    for tb in range(TB):
        par = tb % 2
        for db in range(HB):
            pa = pb0[:, (db % 2) * 512:(db % 2) * 512 + 512]
            k = tb * HB + db
            if k == 0:
                p.wait(t, "tensor", "attn_done")
                p.wait(t, "tensor", "p2_wo")
            gap = 2 if HB >= 2 else 1
            if k >= gap:
                pk = k - gap
                p.wait(t, "tensor", f"p2_aev_{pk // HB}_{pk % HB}")
                p.wait(t, "tensor", f"p2_st_{pk // HB}_{pk % HB}")
            ci = None
            for hc in range(H):
                ci = t.matmul(pa, zy[:, hc * S + tb * 128: hc * S + (tb + 1) * 128],
                              fm[:, hc * D + db * 512: hc * D + (db + 1) * 512],
                              start=(hc == 0), stop=(hc == H - 1))
            p.inc(ci, "tP")
            p.mark(f"p2_attn_{tb}_{db}", "tP")
            if cfg.upto == "p2m":
                p.mark(f"p2_st_{tb}_{db}", "tS")
            else:
                p.wait(v, "vector", f"p2_aev_{tb}_{db}")
                if db == 0 and tb >= 2 and cfg.upto != "p2a":
                    p.wait(v, "vector", f"p2_negmr_{tb-2}")
                ci = v.bn_stats(bnb[:, par * 32 + 6 * db: par * 32 + 6 * db + 6],
                                araw[:, db * 512:(db + 1) * 512])
                p.inc(ci, "tV")
                p.mark(f"p2_st_{tb}_{db}", "tV")
            p.wait(s, "scalar", f"p2_attn_{tb}_{db}")
            if tb >= 2 and db == 0:
                p.wait(s, "scalar", f"p2_x1g_{tb-2}")
            ci = s.copy(araw[:, db * 512:(db + 1) * 512], pa)
            p.inc(ci, "tS")
            p.mark(f"p2_aev_{tb}_{db}", "tS")
        if cfg.upto in ("p2a", "p2m"):
            p.mark(f"p2_x1g_{tb}", "tS")
            p.mark(f"p2_x1_{tb}", "tS")
            p.mark(f"p2_x1st_{tb}", "tS")
            continue
        v.drain()
        ci = v.bn_aggr(mvb[:, par * 2: par * 2 + 2],
                       bnb[:, par * 32: par * 32 + 6 * HB])
        p.inc(ci, "tV")
        p.mark(f"p2_mv_{tb}", "tV")
        p.wait(s, "scalar", f"p2_mv_{tb}")
        s.activation(tln[:, par: par + 1], mvb[:, par * 2 + 1: par * 2 + 2],
                     AF.Ln, bias=epsb[:, 0:1], scale=1.0)
        s.drain()
        ci = s.activation(rstd[:, par: par + 1], tln[:, par: par + 1],
                          AF.Exp, bias=0.0, scale=-0.5)
        p.inc(ci, "tS")
        p.mark(f"p2_rstd_{tb}", "tS")
        s.drain()
        p.wait(v, "vector", f"p2_rstd_{tb}")
        ci = v.tensor_scalar(negmr[:, par: par + 1], mvb[:, par * 2: par * 2 + 1],
                             rstd[:, par: par + 1], -1.0, OP.mult, OP.mult)
        p.inc(ci, "tV")
        p.mark(f"p2_negmr_{tb}", "tV")
        p.wait(s, "scalar", f"p2_negmr_{tb}")
        if tb >= 1:
            p.wait(s, "scalar", f"p2_x1g_{tb-1}")
        ci = s.activation(t1b, araw, AF.Identity,
                          bias=negmr[:, par: par + 1], scale=rstd[:, par: par + 1])
        p.inc(ci, "tS")
        p.mark(f"p2_t1_{tb}", "tS")
        if cfg.upto in ("p2s", "p2t"):
            p.mark(f"p2_x1_{tb}", "tS")
            p.mark(f"p2_x1g_{tb}", "tS")
            p.mark(f"p2_x1st_{tb}", "tS")
            continue
        p.wait(g, "gpsimd", f"p2_t1_{tb}")
        p.wait(g, "gpsimd", f"p2_x0_{tb}")
        ci = g.tensor_tensor(x0b[par], t1b, x0b[par], OP.add)
        p.inc(ci, "tG")
        p.mark(f"p2_x1_{tb}", "tG")
        p.mark(f"p2_x1g_{tb}", "tG")

        p.wait(sy, "sync", f"p2_x1_{tb}")
        ci = sy.dma_start(out=x1_t[tb, :, :], in_=x0b[par])
        p.inc(ci, "dX", 16)
        p.mark(f"p2_x1st_{tb}", "dX")
        p2_load_x0(tb + 2)
    p.mark("p2_done", "tG" if cfg.upto not in ("p2a", "p2m", "p2s", "p2t") else "tS")
    if cfg.upto in ("p2a", "p2m", "p2s", "p2t"):
        if dump == "attnraw":
            p.wait(g, "gpsimd", f"p2_aev_{TB-1}_{HB-1}")
            ci = g.dma_start(out=dump_d, in_=araw)
            p.inc(ci, "dO", 16)
        if not p.dry:
            sy.wait_ge(p.sem["dO"], p.cnt["dO"])
        return

    if dump == "x1":
        p.wait(sy, "sync", "p2_done")
        for tb in range(TB):
            p.wait(sy, "sync", f"p2_x1st_{tb}")
        ci = sy.dma_start(out=dump_d.rearrange("(t p) d -> t p d", p=128), in_=x1_t)
        p.inc(ci, "dO", 16)

    if cfg.upto == "p2":
        if not p.dry:
            sy.wait_ge(p.sem["dO"], p.cnt["dO"])
        return
    for tb in range(TB):
        par = tb % 2
        if tb < 2:
            p.wait(sy, "sync", f"p1_qkvmm_{H-1}")
        else:
            p.wait(sy, "sync", f"p2b_ln2_{tb-2}")
        p.wait(sy, "sync", f"p2_x1st_{tb}")
        ci = sy.dma_start(out=x1b[par], in_=x1_t[tb, :, :])
        p.inc(ci, "dX", 16)
        p.mark(f"p2b_x1_{tb}", "dX")
        p.wait(v, "vector", f"p2b_x1_{tb}")
        if tb >= 2:
            p.wait(v, "vector", f"p2b_rstd_{tb-2}")
        else:
            p.wait(v, "vector", f"p2_negmr_{TB-1}")
        for j in range(D // 512):
            v.bn_stats(bnb[:, par * 32 + 6 * j: par * 32 + 6 * j + 6],
                       x1b[par][:, j * 512:(j + 1) * 512])
        v.drain()
        ci = v.bn_aggr(mvb[:, par * 2: par * 2 + 2],
                       bnb[:, par * 32: par * 32 + 6 * (D // 512)])
        p.inc(ci, "tV")
        p.mark(f"p2b_mv_{tb}", "tV")
        p.wait(s, "scalar", f"p2b_mv_{tb}")
        if tb < 2:
            p.wait(s, "scalar", f"p2_t1_{TB-1}")
        s.activation(tln[:, par: par + 1], mvb[:, par * 2 + 1: par * 2 + 2],
                     AF.Ln, bias=epsb[:, 0:1], scale=1.0)
        s.drain()
        ci = s.activation(rstd[:, par: par + 1], tln[:, par: par + 1],
                          AF.Exp, bias=0.0, scale=-0.5)
        p.inc(ci, "tS")
        p.mark(f"p2b_rstd_{tb}", "tS")
        s.drain()
        p.wait(v, "vector", f"p2b_rstd_{tb}")
        ci = v.tensor_scalar(negmr[:, par: par + 1], mvb[:, par * 2: par * 2 + 1],
                             rstd[:, par: par + 1], -1.0, OP.mult, OP.mult)
        p.inc(ci, "tV")
        p.mark(f"p2b_negmr_{tb}", "tV")
        p.wait(s, "scalar", f"p2b_negmr_{tb}")
        if tb >= 1:
            p.wait(s, "scalar", f"p2b_tr_{tb-1}_{DC//4-1}")
        if tb < 2:
            p.wait(s, "scalar", f"p1_qkvmm_{H-1}")
        ci = s.activation(ln2f, x1b[par], AF.Identity,
                          bias=negmr[:, par: par + 1], scale=rstd[:, par: par + 1])
        p.inc(ci, "tS")
        p.mark(f"p2b_ln2_{tb}", "tS")
        p.wait(t, "tensor", f"p2b_ln2_{tb}")
        if tb == 0:
            p.wait(t, "tensor", f"p2_aev_{TB-1}_{HB-1}")
        for g4 in range(DC // 4):
            k = tb * (DC // 4) + g4
            bpar = k % 2
            bank = pb2 if bpar == 0 else pb1
            if k >= 2:
                pk = k - 2
                p.wait(t, "tensor", f"p2b_rotev_{pk//(DC//4)}_{pk%(DC//4)}")
                p.wait(t, "tensor", f"p2b_cast_{pk//(DC//4)}_{pk%(DC//4)}")
            ci = None
            for j in range(4):
                dc = g4 * 4 + j
                ci = t.transpose(bank[:, j * 128:(j + 1) * 128],
                                 ln2f[:, dc * 128:(dc + 1) * 128], id_f[:])
            p.inc(ci, "tP")
            p.mark(f"p2b_tr_{tb}_{g4}", "tP")
            p.wait(v, "vector", f"p2b_tr_{tb}_{g4}")
            if k >= 2:
                p.wait(v, "vector", f"p2b_gate_{(k-2)//(DC//4)}_{(k-2)%(DC//4)}")
                p.wait(v, "vector", f"p2b_cast_{(k-2)//(DC//4)}_{(k-2)%(DC//4)}")
            ci = v.tensor_copy(rot[bpar], bank[:])
            p.inc(ci, "tV")
            p.mark(f"p2b_rotev_{tb}_{g4}", "tV")
            p.wait(g, "gpsimd", f"p2b_rotev_{tb}_{g4}")
            if tb == 0 and g4 == 0:
                p.wait(g, "gpsimd", "p2_done")
                p.wait(g, "gpsimd", f"p2_attn_{TB-1}_{HB-1}")
            dst = bass.AP(fm, (g4 * 4) * S + tb * 128,
                          [[fm.shape[1], 128], [S, 4], [1, 128]])
            src = rot[bpar].rearrange("p (c n) -> p c n", n=128)
            ci = g.tensor_copy(dst, src)
            p.inc(ci, "tG")
            p.mark(f"p2b_cast_{tb}_{g4}", "tG")
            p.wait(t, "tensor", f"p2b_rotev_{tb}_{g4}")
            p.wait(t, "tensor", "gw")
            if g4 == 0:
                if tb > 0:
                    p.wait(t, "tensor", f"p2b_pexp_{tb-1}")
                else:
                    p.wait(t, "tensor", f"p1_bcev_{H-1}_{QG-1}")
            ci = None
            for j in range(4):
                dc = g4 * 4 + j
                ci = t.matmul(pb4[:, 0:E], rot[bpar][:, j * 128:(j + 1) * 128],
                              gw_sb[:, dc * E:(dc + 1) * E],
                              start=(dc == 0), stop=(dc == DC - 1))
            p.inc(ci, "tP")
            p.mark(f"p2b_gate_{tb}_{g4}", "tP")
        p.mark(f"p2b_fm_{tb}", "tG")
        p.wait(v, "vector", f"p2b_gate_{tb}_{DC//4-1}")
        ci = v.reduce_max(m8[:, 0:1], pb4[:, 0:E], axis=AX.X, negate=True)
        p.inc(ci, "tV")
        p.mark(f"p2b_lmax_{tb}", "tV")
        p.wait(s, "scalar", f"p2b_lmax_{tb}")
        ci = s.activation(pr8[:], pb4[:, 0:E], AF.Exp, bias=m8[:, 0:1],
                          scale=1.0, accum_out=l8[:, 0:1])
        p.inc(ci, "tS")
        p.mark(f"p2b_pexp_{tb}", "tS")
        p.wait(v, "vector", f"p2b_pexp_{tb}")
        v.reciprocal(r8[:], l8[:])
        v.drain()
        v.reduce_max(m8[:, 1:2], pr8[:], axis=AX.X)
        v.drain()
        v.tensor_scalar(mk1[:], pr8[:], m8[:, 1:2], None, OP.is_ge)
        v.drain()
        v.scalar_tensor_tensor(s2b[:], mk1[:], -2.0, pr8[:], OP.mult, OP.add)
        v.drain()
        v.reduce_max(m8[:, 1:2], s2b[:], axis=AX.X)
        v.drain()
        v.tensor_scalar(mk2[:], s2b[:], m8[:, 1:2], None, OP.is_ge)
        v.drain()
        v.tensor_tensor(mk1[:], mk1[:], mk2[:], OP.add)
        v.drain()
        ci = v.scalar_tensor_tensor(comb[:, tb * E:(tb + 1) * E], mk1[:],
                                    r8[:, 0:1], pr8[:], OP.mult, OP.mult)
        p.inc(ci, "tV")
        p.mark(f"p2b_comb_{tb}", "tV")
    p.mark("p2b_done", "tG")
    if p.dry:
        for tb4 in range(TB4):
            tt = min(tb4 * 4 + 3, TB - 1)
            p.marks[f"p2b_fm_ready_{tb4}"] = p.marks[f"p2b_cast_{tt}_{DC//4-1}"]

    if dump == "ln2fm":
        p.wait(g, "gpsimd", f"p2b_fm_{TB-1}")
        p.wait(g, "gpsimd", f"p2b_cast_{TB-1}_{DC//4-1}")
        ci = g.dma_start(out=dump_d.rearrange("(c p) n -> p c n", p=128),
                          in_=fm[:, 0:DC * S].rearrange("p (c n) -> p c n", n=S))
        p.inc(ci, "dO", 16)
    if dump == "comb":
        p.wait(sy, "sync", f"p2b_comb_{TB-1}")
        ci = sy.dma_start(out=dump_d.rearrange("(t p) n -> p t n", p=128),
                          in_=comb[:, 0:TB * E].rearrange("p (t n) -> p t n", n=E))
        p.inc(ci, "dO", 16)

    if cfg.upto == "p2b":
        if not p.dry:
            sy.wait_ge(p.sem["dO"], p.cnt["dO"])
        return
    for ex in range(NEX):
        for fc in range(FC):
            k = ex * FC + fc
            if k < 2:
                p.wait(g, "gpsimd", f"p2b_ln2_{TB-1}")
                p.wait(g, "gpsimd", f"p2b_gate_{TB-1}_{DC//4-1}")
            else:
                p.wait(g, "gpsimd", f"p3_gu_used_{(k-2)//FC}_{(k-2)%FC}")
            ci = g.dma_start(out=gub[k % 2].rearrange("p (c n) -> p c n", n=256),
                             in_=gu_w[ex, fc].rearrange("c p n -> p c n"))
            p.inc(ci, "dW", 16)
            p.mark(f"p3_guw_{ex}_{fc}", "dW")
        for hb2 in range(HB2):
            k = ex * HB2 + hb2
            if k < 2:
                p.wait(g, "gpsimd", f"p2b_ln2_{TB-1}")
                p.wait(g, "gpsimd", f"p2b_gate_{TB-1}_{DC//4-1}")
            else:
                p.wait(g, "gpsimd", f"p3_dn_used_{(k-2)//HB2}_{(k-2)%HB2}")
            src = bass.AP(dn_w.tensor, (ex * FC * 128) * D + hb2 * 256,
                          [[D, 128], [128 * D, FC], [1, 256]])
            ci = g.dma_start(out=dnb[k % 2].rearrange("p (c n) -> p c n", n=256),
                             in_=src)
            p.inc(ci, "dW", 16)
            p.mark(f"p3_dnw_{ex}_{hb2}", "dW")

    ytile = 0
    for ex in range(NEX):
        for fc in range(FC):
            p.wait(t, "tensor", f"p3_guw_{ex}_{fc}")
            if ex == 0 and fc == 0:
                p.wait(t, "tensor", f"p2b_gate_{TB-1}_{DC//4-1}")
                p.wait(t, "tensor", f"p2_aev_{TB-1}_{HB-1}")
            gb = gub[(ex * FC + fc) % 2]
            for tb in range(TB4):
                kk = ex * FC * TB4 + fc * TB4 + tb
                gpar = kk % 2
                if kk >= 2:
                    pk = kk - 2
                    p.wait(t, "tensor",
                           f"p3_act_{pk // (FC*TB4)}_{(pk % (FC*TB4)) // TB4}_{pk % TB4}")
                if ex == 0 and fc == 0 and tb < 2:
                    p.wait(t, "tensor", f"p2b_fm_ready_{tb}")
                for which in range(2):
                    dst = pb0[:, gpar * 1024 + which * 512: gpar * 1024 + (which + 1) * 512]
                    ci = None
                    for dc in range(DC):
                        ci = t.matmul(dst,
                                      gb[:, dc * 256 + which * 128: dc * 256 + which * 128 + 128],
                                      fm[:, dc * S + tb * 512: dc * S + tb * 512 + 512],
                                      start=(dc == 0), stop=(dc == DC - 1))
                    p.inc(ci, "tP")
                p.mark(f"p3_gu_{ex}_{fc}_{tb}", "tP")
                p.wait(s, "scalar", f"p3_gu_{ex}_{fc}_{tb}")
                if kk >= 2:
                    pk = kk - 2
                    p.wait(s, "scalar",
                           f"p3_act_{pk // (FC*TB4)}_{(pk % (FC*TB4)) // TB4}_{pk % TB4}")
                ci = s.activation(s1b[:, gpar * 512:(gpar + 1) * 512],
                                  pb0[:, gpar * 1024: gpar * 1024 + 512],
                                  AF.Silu, bias=0.0, scale=1.0)
                p.inc(ci, "tS")
                p.mark(f"p3_silu_{ex}_{fc}_{tb}", "tS")
                p.wait(v, "vector", f"p3_silu_{ex}_{fc}_{tb}")
                if ex == 0 and fc * TB4 + tb < 2:
                    p.wait(v, "vector", "p2b_done")
                    p.wait(v, "vector", f"p2_x1g_{TB-1}")
                ci = v.tensor_tensor(act[:, fc * S + tb * 512: fc * S + tb * 512 + 512],
                                     pb0[:, gpar * 1024 + 512: gpar * 1024 + 1024],
                                     s1b[:, gpar * 512:(gpar + 1) * 512], OP.mult)
                p.inc(ci, "tV")
                p.mark(f"p3_act_{ex}_{fc}_{tb}", "tV")
            p.mark(f"p3_gu_used_{ex}_{fc}", "tP")
        pyb = [pb1, pb2, pb3, pb4]
        for hb2 in range(HB2):
            p.wait(t, "tensor", f"p3_dnw_{ex}_{hb2}")
            if ex == 0 and hb2 == 0:
                p.wait(t, "tensor", f"p1_rtr_{H-1}_{TB-1}")
                p.wait(t, "tensor", f"p2b_gate_{TB-1}_{DC//4-1}")
            db = dnb[(ex * HB2 + hb2) % 2]
            for tb in range(TB):
                kk = ytile
                if kk >= 4:
                    pk = kk - 4
                    p.wait(t, "tensor",
                           f"p3_yev_{pk // (HB2*TB)}_{(pk % (HB2*TB)) // TB}_{pk % TB}")
                if hb2 == 0 and tb == 0:
                    p.wait(t, "tensor", f"p3_act_{ex}_{FC-1}_{TB4-1}")
                ci = None
                for fc in range(FC):
                    ci = t.matmul(pyb[kk % 4][:, 0:256],
                                  act[:, fc * S + tb * 128: fc * S + (tb + 1) * 128],
                                  db[:, fc * 256:(fc + 1) * 256],
                                  start=(fc == 0), stop=(fc == FC - 1))
                p.inc(ci, "tP")
                p.mark(f"p3_y_{ex}_{hb2}_{tb}", "tP")
                p.wait(v, "vector", f"p3_y_{ex}_{hb2}_{tb}")
                ydst = zy[:, tb * D + hb2 * 256: tb * D + (hb2 + 1) * 256]
                if ex == 0:
                    ci = v.tensor_copy(ydst, pyb[kk % 4][:, 0:256])
                elif ex == 1:
                    ci = v.tensor_tensor(ydst, pyb[kk % 4][:, 0:256], ydst, OP.add)
                else:
                    p.wait(v, "vector", f"p2b_comb_{tb}")
                    ci = v.scalar_tensor_tensor(
                        ydst, pyb[kk % 4][:, 0:256],
                        comb[:, tb * E + (ex - 2): tb * E + (ex - 2) + 1],
                        ydst, OP.mult, OP.add)
                p.inc(ci, "tV")
                p.mark(f"p3_yev_{ex}_{hb2}_{tb}", "tV")
                ytile += 1
            p.mark(f"p3_dn_used_{ex}_{hb2}", "tP")
    p.mark("p3_done", "tV")

    if dump == "act":
        p.wait(g, "gpsimd", f"p3_act_{NEX-1}_{FC-1}_{TB4-1}")
        ci = g.dma_start(out=dump_d.rearrange("(c p) n -> p c n", p=128),
                          in_=act[:, 0:FC * S].rearrange("p (c n) -> p c n", n=S))
        p.inc(ci, "dO", 16)
    if dump == "y":
        p.wait(g, "gpsimd", "p3_done")
        ci = g.dma_start(out=dump_d.rearrange("(c p) n -> p c n", p=128),
                          in_=zy[:, 0:TB * D].rearrange("p (c n) -> p c n", n=D))
        p.inc(ci, "dO", 16)

    if cfg.upto == "p3":
        if not p.dry:
            sy.wait_ge(p.sem["dO"], p.cnt["dO"])
        return
    t2b = wkb[:, 0:D]
    ob = [wkb[:, D + i * D: D + (i + 1) * D] for i in range(2)]
    x1r = [wkb[:, 3 * D + i * D: 3 * D + (i + 1) * D] for i in range(2)]

    for tb in range(TB):
        par = tb % 2
        if tb < 2:
            p.wait(sy, "sync", "p3_done")
        else:
            p.wait(sy, "sync", f"p4_out_{tb-2}")
        ci = sy.dma_start(out=x1r[par], in_=x1_t[tb, :, :])
        p.inc(ci, "dX", 16)
        p.mark(f"p4_x1_{tb}", "dX")
        p.wait(v, "vector", f"p3_yev_{NEX-1}_{HB2-1}_{tb}")
        if tb < 2:
            p.wait(v, "vector", f"p2b_comb_{TB-1}")
        if tb >= 2:
            p.wait(v, "vector", f"p4_rstd_{tb-2}")
        yv = zy[:, tb * D:(tb + 1) * D]
        for j in range(D // 512):
            v.bn_stats(bnb[:, par * 32 + 6 * j: par * 32 + 6 * j + 6],
                       yv[:, j * 512:(j + 1) * 512])
        v.drain()
        ci = v.bn_aggr(mvb[:, par * 2: par * 2 + 2],
                       bnb[:, par * 32: par * 32 + 6 * (D // 512)])
        p.inc(ci, "tV")
        p.mark(f"p4_mv_{tb}", "tV")
        p.wait(s, "scalar", f"p4_mv_{tb}")
        s.activation(tln[:, par: par + 1], mvb[:, par * 2 + 1: par * 2 + 2],
                     AF.Ln, bias=epsb[:, 0:1], scale=1.0)
        s.drain()
        ci = s.activation(rstd[:, par: par + 1], tln[:, par: par + 1],
                          AF.Exp, bias=0.0, scale=-0.5)
        p.inc(ci, "tS")
        p.mark(f"p4_rstd_{tb}", "tS")
        s.drain()
        p.wait(v, "vector", f"p4_rstd_{tb}")
        ci = v.tensor_scalar(negmr[:, par: par + 1], mvb[:, par * 2: par * 2 + 1],
                             rstd[:, par: par + 1], -1.0, OP.mult, OP.mult)
        p.inc(ci, "tV")
        p.mark(f"p4_negmr_{tb}", "tV")
        p.wait(s, "scalar", f"p4_negmr_{tb}")
        if tb < 2:
            p.wait(s, "scalar", f"p3_y_{NEX-1}_{HB2-1}_{TB-1}")
        if tb >= 1:
            p.wait(s, "scalar", f"p4_out_{tb-1}")
        ci = s.activation(t2b, yv, AF.Identity,
                          bias=negmr[:, par: par + 1], scale=rstd[:, par: par + 1])
        p.inc(ci, "tS")
        p.mark(f"p4_t2_{tb}", "tS")
        p.wait(g, "gpsimd", f"p4_t2_{tb}")
        p.wait(g, "gpsimd", f"p4_x1_{tb}")
        if tb >= 2:
            p.wait(g, "gpsimd", f"p4_outst_{tb-2}")
        ci = g.tensor_tensor(ob[par], t2b, x1r[par], OP.add)
        p.inc(ci, "tG")
        p.mark(f"p4_out_{tb}", "tG")
        p.wait(sy, "sync", f"p4_out_{tb}")
        ci = sy.dma_start(out=out_t[tb, :, :], in_=ob[par])
        p.inc(ci, "dO", 16)
        p.mark(f"p4_outst_{tb}", "dO")

    if not p.dry:
        sy.wait_ge(p.sem["dO"], p.cnt["dO"])



def build_program(cfg: Cfg):
    S, D, H, E, F = cfg.S, cfg.D, cfg.H, cfg.E, cfg.F
    DC, TB, FC, NEX = cfg.DC, cfg.TB, cfg.FC, cfg.NEX
    nc = bass.Bass("TRN2", target_bir_lowering=False)

    x_in = nc.dram_tensor("x", [S, D], F32, kind="ExternalInput").ap()
    wqkv = nc.dram_tensor("wqkv", [H, DC, 128, 384], F32, kind="ExternalInput").ap()
    wo = nc.dram_tensor("wo", [H, 128, D], F32, kind="ExternalInput").ap()
    gatew = nc.dram_tensor("gatew", [DC, 128, E], F32, kind="ExternalInput").ap()
    gu_w = nc.dram_tensor("gu_w", [NEX, FC, DC, 128, 256], F32, kind="ExternalInput").ap()
    dn_w = nc.dram_tensor("dn_w", [NEX, FC, 128, D], F32, kind="ExternalInput").ap()
    out_d = nc.dram_tensor("out", [S, D], F32, kind="ExternalOutput").ap()
    x1_d = nc.dram_tensor("x1scr", [S, D], F32, kind="Internal").ap()
    dump_d = None
    if cfg.dump is not None:
        shp = {"ln1fm": [DC * 128, S], "zt": [H * 128, S], "x1": [S, D],
               "ln2fm": [DC * 128, S], "comb": [S, E], "act": [FC * 128, S],
               "y": [S, D], "qkvh0": [3 * 128, S], "prh0": [2 * 128, S],
               "zth0": [128, S], "prtend": [128, 2 * S],
               "bcend": [128, 256], "attnraw": [128, D],
               "wofm": [None, D]}[cfg.dump]
        if cfg.dump == "wofm":
            shp = [cfg.H * 128, D]
        dump_d = nc.dram_tensor("dump", shp, F32, kind="ExternalOutput").ap()

    x_t = x_in.rearrange("(t p) d -> t p d", p=128)
    x1_t = x1_d.rearrange("(t p) d -> t p d", p=128)
    out_t = out_d.rearrange("(t p) d -> t p d", p=128)
    io = (x_t, x1_t, out_t, wqkv, wo, gatew, gu_w, dn_w, dump_d)

    with ExitStack() as st:
        p = P(nc, st)
        sb = lambda n, sh, dt: st.enter_context(nc.sbuf_tensor(n, sh, dt))
        fm = sb("fm", [128, max(DC, H) * S], BF16)
        zy = sb("zy", [128, max(H, TB * D // S) * S], BF16)
        wk = sb("wk", [128, max(9 * S + 1792, FC * S, 10 * D)], BF16)
        wg = sb("wg", [128, max(2 * DC * 384, 2 * DC * 256 + 2 * FC * 256, 6 * D + 2048)], BF16)
        comb = sb("comb", [128, TB * E], F32)
        gw_sb = sb("gw_sb", [128, DC * E], F32)
        ones = sb("ones", [128, 128], F32)
        id_f = sb("id_f", [128, 128], F32)
        id_b = sb("id_b", [128, 128], BF16)
        mask_b = sb("mask_b", [128, 128], BF16)
        bnb = sb("bnb", [128, 64], F32)
        mvb = sb("mvb", [128, 4], F32)
        tln = sb("tln", [128, 2], F32)
        rstd = sb("rstd", [128, 2], F32)
        negmr = sb("negmr", [128, 2], F32)
        lsum = sb("lsum", [128, 2], F32)
        rcol = sb("rcol", [128, 2], F32)
        nmax = sb("nmax", [128, 2], F32)
        rrow = sb("rrow", [1, 256], F32)
        s1b = sb("s1b", [128, 1024], BF16)
        pr8 = sb("pr8", [128, E], F32)
        s2b = sb("s2b", [128, E], F32)
        mk1 = sb("mk1", [128, E], F32)
        mk2 = sb("mk2", [128, E], F32)
        l8 = sb("l8", [128, 1], F32)
        r8 = sb("r8", [128, 1], F32)
        m8 = sb("m8", [128, 2], F32)
        epsb = sb("epsb", [128, 1], F32)
        pb0 = st.enter_context(nc.psum_tensor("pb0", [128, 2048], F32))
        pb1 = st.enter_context(nc.psum_tensor("pb1", [128, 512], F32))
        pb2 = st.enter_context(nc.psum_tensor("pb2", [128, 512], F32))
        pb3 = st.enter_context(nc.psum_tensor("pb3", [128, 512], F32))
        pb4 = st.enter_context(nc.psum_tensor("pb4", [128, 512], F32))
        buf = (fm, zy, wk, wg, comb, gw_sb, ones, id_f, id_b, mask_b,
               bnb, mvb, tln, rstd, negmr, lsum, rcol, nmax, rrow, s1b,
               pr8, s2b, mk1, mk2, l8, r8, m8, epsb,
               pb0, pb1, pb2, pb3, pb4)
        block = st.enter_context(nc.Block())

        fake = {n: FakeEng() for n in ("sync", "gpsimd", "tensor", "vector", "scalar")}
        p.reset(dry=True)
        emit(p, cfg, fake, io, buf)
        for name in ("sync", "gpsimd", "tensor", "vector", "scalar"):
            sec = getattr(block, name)

            def body(engobj, name=name):
                eng = dict(fake)
                eng[name] = engobj
                p.reset(dry=False)
                emit(p, cfg, eng, io, buf)

            sec(body)
    return nc


def _layout(WK, WQ, WV, WO, GW, GUW, DNW, SGU, SDN, cfg: Cfg):
    H, D, E, F, FC, DC = cfg.H, cfg.D, cfg.E, cfg.F, cfg.FC, cfg.DC
    DH, FS = 128, cfg.FS
    f32 = np.float32
    tr = lambda w: np.ascontiguousarray(np.transpose(np.asarray(w, f32), (0, 2, 1)))
    wqt = tr(WQ).reshape(H, DC, 128, DH)
    wkt = tr(WK).reshape(H, DC, 128, DH)
    wvt = tr(WV).reshape(H, DC, 128, DH)
    wqkv = np.ascontiguousarray(np.concatenate([wqt, wkt, wvt], axis=-1))
    wo = np.ascontiguousarray(np.asarray(WO, f32).T.reshape(H, 128, D))
    gatew = np.ascontiguousarray(np.asarray(GW, f32).T.reshape(DC, 128, E))

    def gu_slabs(gT):
        nf = gT.shape[-1] // 2
        fc = nf // 128
        gg = gT[..., :nf].reshape(-1, DC, 128, fc, 128).transpose(0, 3, 1, 2, 4)
        uu = gT[..., nf:].reshape(-1, DC, 128, fc, 128).transpose(0, 3, 1, 2, 4)
        return np.concatenate([gg, uu], axis=-1)

    guT = tr(GUW)
    gue = gu_slabs(guT)
    sgT = np.asarray(SGU, f32).T
    sa = np.concatenate([sgT[:, 0:F], sgT[:, FS:FS + F]], axis=1)
    sb_ = np.concatenate([sgT[:, F:FS], sgT[:, FS + F:2 * FS]], axis=1)
    gus = gu_slabs(np.stack([sa, sb_]))
    gu_all = np.ascontiguousarray(np.concatenate([gus, gue], axis=0))

    dne = tr(DNW).reshape(E, FC, 128, D)
    dns = np.ascontiguousarray(np.asarray(SDN, f32).T).reshape(2, FC, 128, D)
    dn_all = np.ascontiguousarray(np.concatenate([dns, dne], axis=0))
    return {"wqkv": wqkv, "wo": wo, "gatew": gatew, "gu_w": gu_all, "dn_w": dn_all}


def kernel(L0_attn_x, L0_attn_W_K, L0_attn_W_Q, L0_attn_W_V, L0_attn_W_O,
           L0_ffn_gate_weight, L0_ffn_gate_up_weights, L0_ffn_down_weights,
           L0_ffn_shared_gate_up, L0_ffn_shared_down):
    cfg = Cfg()
    w = _layout(L0_attn_W_K, L0_attn_W_Q, L0_attn_W_V, L0_attn_W_O,
                L0_ffn_gate_weight, L0_ffn_gate_up_weights, L0_ffn_down_weights,
                L0_ffn_shared_gate_up, L0_ffn_shared_down, cfg)
    x = np.asarray(L0_attn_x, np.float32)
    nc = build_program(cfg)
    in_maps = []
    for b in range(x.shape[0]):
        m = dict(w)
        m["x"] = np.ascontiguousarray(x[b])
        in_maps.append(m)
    res = run_bass_kernel_spmd(nc, in_maps, core_ids=list(range(len(in_maps))))
    return np.stack([np.asarray(r["out"]) for r in res.results], axis=0)


# version 13 (final)
# speedup vs baseline: 1.0310x; 1.0310x over previous
import math
from contextlib import ExitStack

import numpy as np
import concourse.bass as bass
import concourse.mybir as mybir
from concourse.alu_op_type import AluOpType
from concourse.bass_utils import run_bass_kernel_spmd

F32 = mybir.dt.float32
BF16 = mybir.dt.bfloat16
AF = mybir.ActivationFunctionType
AX = mybir.AxisListType
OP = AluOpType

NEG = -1.0e10
EPS = 1.0e-5


class Cfg:
    def __init__(self, S=2048, D=2048, H=16, E=8, F=1408, dump=None, upto="p4"):
        self.S, self.D, self.H, self.E, self.F = S, D, H, E, F
        self.upto = upto
        self.DH = 128
        self.FS = 2 * F
        self.DC = D // 128
        self.TB = S // 128
        self.TB4 = S // 512
        self.QG = self.TB // 2
        self.FC = F // 128
        self.HB = D // 512
        self.HB2 = D // 256
        self.NEX = E + 2
        self.dump = dump
        assert D == H * 128 and S % 512 == 0 and D % 512 == 0 and F % 128 == 0


class FakeInstr:
    def then_inc(self, *_a, **_k):
        return self


class FakeEng:
    def __getattr__(self, name):
        def f(*_a, **_k):
            return FakeInstr()
        return f


class P:

    def __init__(self, nc, stack):
        self.nc = nc
        self.sem = {}
        for n in ("dW", "dX", "dO", "tP", "tV", "tS", "tG"):
            self.sem[n] = stack.enter_context(nc.semaphore(n))
        self.reset(dry=True)
        self.marks = {}

    def reset(self, dry):
        self.dry = dry
        self.cnt = {n: 0 for n in ("dW", "dX", "dO", "tP", "tV", "tS", "tG")}
        self.seen = {}

    def inc(self, instr, s, n=1):
        self.cnt[s] += n
        if not self.dry:
            instr.then_inc(self.sem[s], n)
        return self.cnt[s]

    def mark(self, name, s):
        if self.dry:
            self.marks[name] = (s, self.cnt[s])

    def wait(self, eng, engid, name):
        if self.dry:
            return
        s, v = self.marks[name]
        if v <= 0:
            return
        key = (engid, s)
        if self.seen.get(key, -1) >= v:
            return
        eng.wait_ge(self.sem[s], v)
        self.seen[key] = v

    def wait_cnt(self, eng, engid, s, v):
        if self.dry or v <= 0:
            return
        key = (engid, s)
        if self.seen.get(key, -1) >= v:
            return
        eng.wait_ge(self.sem[s], v)
        self.seen[key] = v


def emit(p, cfg, eng, io, buf):
    S, D, H, E, F = cfg.S, cfg.D, cfg.H, cfg.E, cfg.F
    DC, TB, TB4, QG, FC, HB, HB2, NEX = (cfg.DC, cfg.TB, cfg.TB4, cfg.QG,
                                         cfg.FC, cfg.HB, cfg.HB2, cfg.NEX)
    sy, g, t, v, s = eng["sync"], eng["gpsimd"], eng["tensor"], eng["vector"], eng["scalar"]
    (x_t, x1_t, out_t, wqkv, wo, gatew, gu_w, dn_w, dump_d) = io
    (fm, zy, wk, wg, comb, gw_sb, ones, id_f, id_b, mask_b,
     bnb, mvb, tln, rstd, negmr, lsum, rcol, nmax, rrow, s1b,
     pr8, s2b, mk1, mk2, l8, r8, m8, epsb,
     pb0, pb1, pb2, pb3, pb4) = buf

    wkb = wk[:].bitcast(F32)
    wgb = wg[:].bitcast(F32)
    x0b = [wkb[:, i * D:(i + 1) * D] for i in range(2)]
    araw = wkb[:, 2 * D:3 * D]
    t1b = wkb[:, 3 * D:4 * D]
    lnt = [wk[:, 8 * D + i * D: 8 * D + (i + 1) * D] for i in range(2)]
    qh = wk[:, 0:S]
    kh = wk[:, S:2 * S]
    vtm = wk[:, 2 * S:3 * S]
    prb = [wk[:, 3 * S + i * S: 3 * S + (i + 1) * S] for i in range(2)]
    prT = wk[:, 5 * S:7 * S]
    vfm = [wk[:, 7 * S + i * 512: 7 * S + (i + 1) * 512] for i in range(2)]
    bc = wkb[:, (7 * S + 1024) // 2: (7 * S + 1024) // 2 + 256]
    act = wk
    wqb = [wg[:, i * DC * 384:(i + 1) * DC * 384] for i in range(2)]
    x1b = [wgb[:, i * D:(i + 1) * D] for i in range(2)]
    ln2f = wgb[:, 2 * D:3 * D]
    rot = [wgb[:, 3 * D + i * 512: 3 * D + (i + 1) * 512] for i in range(2)]
    gub = [wg[:, i * DC * 256:(i + 1) * DC * 256] for i in range(2)]
    dnb = [wg[:, 2 * DC * 256 + i * FC * 256: 2 * DC * 256 + (i + 1) * FC * 256]
           for i in range(2)]
    pb1h = pb1[:].bitcast(BF16)
    pb2h = pb2[:].bitcast(BF16)

    DH = 128
    dump = cfg.dump
    H_RUN = 1 if cfg.upto == "p1h0" else H

    ci = sy.dma_start(out=gw_sb[:].rearrange("p (c n) -> p c n", n=E),
                      in_=gatew.rearrange("c p n -> p c n"))
    p.inc(ci, "dX", 16)
    p.mark("gw", "dX")

    ci = g.memset(ones[:], 1.0)
    p.inc(ci, "tG")
    ci = g.memset(mask_b[:], 0.0)
    p.inc(ci, "tG")
    ci = g.memset(epsb[:], EPS)
    p.inc(ci, "tG")
    p.wait_cnt(g, "gpsimd", "tG", p.cnt["tG"] if not p.dry else 0) if False else None
    c0 = p.cnt["tG"]
    if not p.dry:
        g.wait_ge(p.sem["tG"], c0)
    ci = g.affine_select(id_f[:], ones[:], [[-1, 128]], OP.is_equal, 0.0,
                        base=0, channel_multiplier=1)
    p.inc(ci, "tG")
    c1 = p.cnt["tG"]
    if not p.dry:
        g.wait_ge(p.sem["tG"], c1)
    ci = g.tensor_copy(id_b[:], id_f[:])
    p.inc(ci, "tG")
    ci = g.affine_select(mask_b[:], mask_b[:], [[-1, 128]], OP.is_ge, NEG,
                         base=0, channel_multiplier=1)
    p.inc(ci, "tG")
    c2 = p.cnt["tG"]
    if not p.dry:
        g.wait_ge(p.sem["tG"], c2)
    p.mark("const", "tG")

    TRG = 8
    n_tr_groups = (DC + TRG - 1) // TRG

    x0q = [x0b[0], x0b[1], araw, t1b]
    for tb in range(TB):
        par = tb % 2
        if tb >= 4:
            p.wait(sy, "sync", f"p0_ln_{tb-4}")
        ci = sy.dma_start(out=x0q[tb % 4], in_=x_t[tb, :, :])
        p.inc(ci, "dX", 16)
        p.mark(f"p0_x0_{tb}", "dX")
        p.wait(v, "vector", f"p0_x0_{tb}")
        if tb >= 2:
            p.wait(v, "vector", f"p0_rstd_{tb-2}")
        for j in range(D // 512):
            v.bn_stats(bnb[:, par * 32 + 6 * j: par * 32 + 6 * j + 6],
                       x0q[tb % 4][:, j * 512:(j + 1) * 512])
        v.drain()
        ci = v.bn_aggr(mvb[:, par * 2: par * 2 + 2],
                       bnb[:, par * 32: par * 32 + 6 * (D // 512)])
        p.inc(ci, "tV")
        p.mark(f"p0_mv_{tb}", "tV")
        p.wait(s, "scalar", f"p0_mv_{tb}")
        p.wait(s, "scalar", "const")
        s.activation(tln[:, par: par + 1], mvb[:, par * 2 + 1: par * 2 + 2],
                     AF.Ln, bias=epsb[:, 0:1], scale=1.0)
        s.drain()
        ci = s.activation(rstd[:, par: par + 1], tln[:, par: par + 1],
                          AF.Exp, bias=0.0, scale=-0.5)
        p.inc(ci, "tS")
        p.mark(f"p0_rstd_{tb}", "tS")
        s.drain()
        p.wait(v, "vector", f"p0_rstd_{tb}")
        ci = v.tensor_scalar(negmr[:, par: par + 1], mvb[:, par * 2: par * 2 + 1],
                             rstd[:, par: par + 1], -1.0, OP.mult, OP.mult)
        p.inc(ci, "tV")
        p.mark(f"p0_negmr_{tb}", "tV")
        p.wait(s, "scalar", f"p0_negmr_{tb}")
        if tb >= 2:
            p.wait(s, "scalar", f"p0_tr_{tb-2}_{n_tr_groups-1}")
        ci = s.activation(lnt[par], x0q[tb % 4], AF.Identity,
                          bias=negmr[:, par: par + 1], scale=rstd[:, par: par + 1])
        p.inc(ci, "tS")
        p.mark(f"p0_ln_{tb}", "tS")
        p.wait(t, "tensor", "const")
        p.wait(t, "tensor", f"p0_ln_{tb}")
        for gi in range(n_tr_groups):
            ctr = tb * n_tr_groups + gi
            bpar = ctr % 2
            ptr = pb1h if bpar == 0 else pb2h
            nt = min(TRG, DC - gi * TRG)
            if ctr >= 2:
                pc = ctr - 2
                p.wait(t, "tensor", f"p0_ev_{pc // n_tr_groups}_{pc % n_tr_groups}")
            ci = None
            for j in range(nt):
                dc = gi * TRG + j
                ci = t.transpose(ptr[:, j * 128:(j + 1) * 128],
                                 lnt[par][:, dc * 128:(dc + 1) * 128], id_b[:])
            p.inc(ci, "tP")
            p.mark(f"p0_tr_{tb}_{gi}", "tP")
            p.wait(v, "vector", f"p0_tr_{tb}_{gi}")
            src = ptr[:, 0:nt * 128].rearrange("p (c n) -> p c n", n=128)
            dst = bass.AP(fm, gi * TRG * S + tb * 128,
                          [[fm.shape[1], 128], [S, nt], [1, 128]])
            ci = v.tensor_copy(dst, src)
            p.inc(ci, "tV")
            p.mark(f"p0_ev_{tb}_{gi}", "tV")
    p.mark("ln1_done", "tV")

    if dump == "ln1fm":
        p.wait(g, "gpsimd", "ln1_done")
        ci = g.dma_start(out=dump_d.rearrange("(c p) n -> p c n", p=128),
                          in_=fm[:, 0:DC * S].rearrange("p (c n) -> p c n", n=S))
        p.inc(ci, "dO", 16)
    if cfg.upto == "p0":
        if not p.dry:
            sy.wait_ge(p.sem["dO"], p.cnt["dO"])
        return

    for h in range(H_RUN):
        if h >= 2:
            p.wait(g, "gpsimd", f"p1_qkvmm_{h-2}")
        ci = g.dma_start(out=wqb[h % 2].rearrange("p (c n) -> p c n", n=384),
                         in_=wqkv[h].rearrange("c p n -> p c n"))
        p.inc(ci, "dW", 16)
        p.mark(f"p1_w_{h}", "dW")

    trpar = [0]
    last_tr_ev = [None, None]

    def tr_group(eng_t, srcs, dst_builder, name):
        par = trpar[0] % 2
        bank = pb1 if par == 0 else pb2
        bankh = pb1h if par == 0 else pb2h
        if last_tr_ev[par] is not None:
            p.wait(t, "tensor", last_tr_ev[par])
        ci = None
        for j, src in enumerate(srcs):
            ci = t.transpose(bankh[:, j * 128:(j + 1) * 128], src, id_b[:])
        p.inc(ci, "tP")
        p.mark(f"{name}_tr", "tP")
        p.wait(v, "vector", f"{name}_tr")
        src = bankh[:, 0:len(srcs) * 128].rearrange("p (c n) -> p c n", n=128)
        ci = v.tensor_copy(dst_builder(len(srcs)), src)
        p.inc(ci, "tV")
        p.mark(f"{name}_ev", "tV")
        last_tr_ev[par] = f"{name}_ev"
        trpar[0] += 1

    for h in range(H_RUN):
        p.wait(t, "tensor", "ln1_done")
        p.wait(t, "tensor", f"p1_w_{h}")
        for tb in range(TB4):
            if tb > 0:
                p.wait(t, "tensor", f"p1_q_{h}_{tb-1}")
                p.wait(t, "tensor", f"p1_kv_{h}_{tb-1}")
            elif h > 0:
                p.wait(t, "tensor", f"p1_exp_{h-1}_{TB-1}")
            for which in range(3):
                dst = pb0[:, which * 512:(which + 1) * 512]
                ci = None
                for dc in range(DC):
                    ci = t.matmul(dst,
                                  wqb[h % 2][:, dc * 384 + which * DH: dc * 384 + (which + 1) * DH],
                                  fm[:, dc * S + tb * 512: dc * S + tb * 512 + 512],
                                  start=(dc == 0), stop=(dc == DC - 1))
                p.inc(ci, "tP")
                p.mark(f"p1_qkv_{h}_{tb}_{which}", "tP")
            p.wait(s, "scalar", f"p1_qkv_{h}_{tb}_0")
            if h > 0 and tb == 0:
                p.wait(s, "scalar", f"p1_sc_{h-1}_{TB-1}")
            ci = s.activation(qh[:, tb * 512:(tb + 1) * 512],
                              pb0[:, 0:512], AF.Copy, bias=0.0,
                              scale=1.0 / math.sqrt(DH))
            p.inc(ci, "tS")
            p.mark(f"p1_q_{h}_{tb}", "tS")
            p.wait(v, "vector", f"p1_qkv_{h}_{tb}_1")
            p.wait(v, "vector", f"p1_qkv_{h}_{tb}_2")
            if h > 0 and tb == 0:
                p.wait(v, "vector", f"p1_sc_{h-1}_{TB-1}")
            if tb >= 2:
                p.wait(v, "vector", f"p1_vtr_{h}_{tb-2}_tr")
            elif h > 0:
                p.wait(v, "vector", f"p1_vtr_{h-1}_{TB4-2}_tr" if TB4 >= 2 else f"p1_vtr_{h-1}_{TB4-1}_tr")
            v.tensor_copy(kh[:, tb * 512:(tb + 1) * 512], pb0[:, 512:1024])
            ci = v.tensor_copy(vfm[tb % 2], pb0[:, 1024:1536])
            p.inc(ci, "tV")
            p.mark(f"p1_kv_{h}_{tb}", "tV")
            if h > 0 and tb == 0:
                p.wait(t, "tensor", f"p1_av_{h-1}_{QG-1}")
            p.wait(t, "tensor", f"p1_kv_{h}_{tb}")
            tr_group(t, [vfm[tb % 2][:, j * 128:(j + 1) * 128] for j in range(4)],
                     lambda n, tb=tb: bass.AP(wk, 2 * S + tb * 512,
                                              [[wk.shape[1], 128], [128, n], [1, 128]]),
                     f"p1_vtr_{h}_{tb}")
        p.mark(f"p1_qkvmm_{h}", "tP")

        def attn_scores(qt):
            par2 = qt % 2
            W_ = (qt + 1) * 128
            if qt < 8:
                off = par2 * 1024
                if qt >= 2:
                    p.wait(t, "tensor", f"p1_exp_{h}_{qt-2}")
                elif h > 0:
                    p.wait(t, "tensor", f"p1_exp_{h-1}_{TB-1}")
            else:
                off = 0
                p.wait(t, "tensor", f"p1_exp_{h}_{qt-1}")
            p.wait(t, "tensor", f"p1_q_{h}_{min(qt // 4, TB4 - 1)}")
            p.wait(t, "tensor", f"p1_kv_{h}_{min(qt // 4, TB4 - 1)}")
            if qt == 0:
                p.wait(t, "tensor", f"p1_q_{h}_{TB4-1}")
                p.wait(t, "tensor", f"p1_kv_{h}_{TB4-1}")
            nfull = (qt * 128) // 512
            for g5 in range(nfull):
                t.matmul(pb0[:, off + g5 * 512: off + (g5 + 1) * 512],
                         qh[:, qt * 128:(qt + 1) * 128],
                         kh[:, g5 * 512:(g5 + 1) * 512], start=True, stop=True)
            rem = qt * 128 - nfull * 512
            if rem > 0:
                t.matmul(pb0[:, off + nfull * 512: off + qt * 128],
                         qh[:, qt * 128:(qt + 1) * 128],
                         kh[:, nfull * 512: qt * 128], start=True, stop=True)
            t.matmul(pb0[:, off + qt * 128: off + (qt + 1) * 128],
                     qh[:, qt * 128:(qt + 1) * 128],
                     kh[:, qt * 128:(qt + 1) * 128], start=True, stop=False)
            ci = t.matmul(pb0[:, off + qt * 128: off + (qt + 1) * 128],
                          id_b[:], mask_b[:], start=False, stop=True)
            p.inc(ci, "tP")
            p.mark(f"p1_sc_{h}_{qt}", "tP")
            p.wait(v, "vector", f"p1_sc_{h}_{qt}")
            if qt >= 2:
                p.wait(v, "vector", f"p1_exp_{h}_{qt-2}")
            ci = v.reduce_max(nmax[:, par2: par2 + 1], pb0[:, off: off + W_],
                              axis=AX.X, negate=True)
            p.inc(ci, "tV")
            p.mark(f"p1_max_{h}_{qt}", "tV")
            p.wait(s, "scalar", f"p1_max_{h}_{qt}")
            if qt >= 2:
                p.wait(s, "scalar", f"p1_ptr_{h}_{qt-2}_{(qt-2)//TRG}_tr")
            elif h > 0:
                lq = TB - 2 + par2
                p.wait(s, "scalar", f"p1_ptr_{h-1}_{lq}_{lq//TRG}_tr")
            ci = s.activation(prb[par2][:, 0:W_],
                              pb0[:, off: off + W_], AF.Exp,
                              bias=nmax[:, par2: par2 + 1], scale=1.0,
                              accum_out=lsum[:, par2: par2 + 1])
            p.inc(ci, "tS")
            p.mark(f"p1_exp_{h}_{qt}", "tS")

        def attn_post(qt):
            par2 = qt % 2
            qg = qt // 2
            p.wait(v, "vector", f"p1_exp_{h}_{qt}")
            if qt >= 2:
                p.wait(v, "vector", f"p1_rtr_{h}_{qt-2}")
            ci = v.reciprocal(rcol[:, par2: par2 + 1], lsum[:, par2: par2 + 1])
            p.inc(ci, "tV")
            p.mark(f"p1_rcol_{h}_{qt}", "tV")
            p.wait(t, "tensor", f"p1_exp_{h}_{qt}")
            if par2 == 0:
                if qg > 0:
                    p.wait(v, "vector", f"p1_av_{h}_{qg-1}")
                elif h > 0:
                    p.wait(v, "vector", f"p1_av_{h-1}_{QG-1}")
                ci = v.memset(prT[:, (2 * qg + 1) * 256:(2 * qg + 1) * 256 + 128], 0.0)
                p.inc(ci, "tV")
                p.mark(f"p1_prz_{h}_{qg}", "tV")
            nkt = qt + 1
            for gi in range((nkt + TRG - 1) // TRG):
                nt = min(TRG, nkt - gi * TRG)
                if par2 == 0 and gi == 0 and qg > 0:
                    p.wait(t, "tensor", f"p1_av_{h}_{qg-1}")
                srcs = [prb[par2][:, (gi * TRG + j) * 128:(gi * TRG + j + 1) * 128]
                        for j in range(nt)]
                tr_group(t, srcs,
                         lambda n, gi=gi, par2=par2: bass.AP(
                             wk, 5 * S + gi * TRG * 256 + par2 * 128,
                             [[wk.shape[1], 128], [256, n], [1, 128]]),
                         f"p1_ptr_{h}_{qt}_{gi}")
            p.wait(t, "tensor", f"p1_rcol_{h}_{qt}")
            if par2 == 0:
                if qg > 0:
                    p.wait(t, "tensor", f"p1_bcev_{h}_{qg-1}")
                elif h > 0:
                    p.wait(t, "tensor", f"p1_bcev_{h-1}_{QG-1}")
            ci = t.transpose(pb4[0:1, par2 * 128:(par2 + 1) * 128],
                             rcol[:, par2: par2 + 1], id_f[:])
            p.inc(ci, "tP")
            p.mark(f"p1_rtr_{h}_{qt}", "tP")
            if par2 == 0:
                return
            p.wait(s, "scalar", f"p1_rtr_{h}_{2*qg+1}")
            if qg > 0:
                p.wait(s, "scalar", f"p1_bc_{h}_{qg-1}")
            elif h > 0:
                p.wait(s, "scalar", f"p1_bc_{h-1}_{QG-1}")
            ci = s.copy(rrow[0:1, :], pb4[0:1, 0:256])
            p.inc(ci, "tS")
            p.mark(f"p1_rrowev_{h}_{qg}", "tS")
            p.wait(t, "tensor", f"p1_rrowev_{h}_{qg}")
            if qg > 0:
                p.wait(t, "tensor", f"p1_bcev_{h}_{qg-1}")
            elif h > 0:
                p.wait(t, "tensor", f"p1_bcev_{h-1}_{QG-1}")
            ci = t.matmul(pb4[:, 256:512], ones[0:1, :], rrow[0:1, :],
                          start=True, stop=True)
            p.inc(ci, "tP")
            p.mark(f"p1_bc_{h}_{qg}", "tP")
            p.wait(v, "vector", f"p1_bc_{h}_{qg}")
            ci = v.tensor_copy(bc, pb4[:, 256:512])
            p.inc(ci, "tV")
            p.mark(f"p1_bcev_{h}_{qg}", "tV")
            nkt2 = 2 * qg + 2
            p.wait(t, "tensor", f"p1_ptr_{h}_{2*qg+1}_{(nkt2-1)//TRG}_ev")
            p.wait(t, "tensor", f"p1_prz_{h}_{qg}")
            if qg > 0:
                p.wait(t, "tensor", f"p1_z_{h}_{qg-1}")
            elif h > 0:
                p.wait(t, "tensor", f"p1_z_{h-1}_{QG-1}")
            ci = None
            for kt in range(nkt2):
                ci = t.matmul(pb3[:, 0:256], vtm[:, kt * 128:(kt + 1) * 128],
                              prT[:, kt * 256:(kt + 1) * 256],
                              start=(kt == 0), stop=(kt == nkt2 - 1))
            p.inc(ci, "tP")
            p.mark(f"p1_av_{h}_{qg}", "tP")
            p.wait(v, "vector", f"p1_av_{h}_{qg}")
            ci = v.tensor_tensor(zy[:, h * S + qg * 256: h * S + (qg + 1) * 256],
                                 pb3[:, 0:256], bc, OP.mult)
            p.inc(ci, "tV")
            p.mark(f"p1_z_{h}_{qg}", "tV")

        for qt in range(TB):
            attn_scores(qt)
            if qt >= 1:
                attn_post(qt - 1)
        attn_post(TB - 1)
    p.mark("attn_done", "tV")
    if dump == "prtend":
        p.wait(g, "gpsimd", "attn_done")
        ci = g.dma_start(out=dump_d, in_=prT)
        p.inc(ci, "dO", 16)
    if dump == "bcend":
        p.wait(g, "gpsimd", "attn_done")
        ci = g.dma_start(out=dump_d, in_=bc)
        p.inc(ci, "dO", 16)
    if dump == "zth0":
        p.wait(g, "gpsimd", f"p1_z_0_{QG-1}")
        ci = g.dma_start(out=dump_d, in_=zy[:, 0:S])
        p.inc(ci, "dO", 16)
    if dump == "qkvh0":
        p.wait(g, "gpsimd", "p1_q_0_" + str(TB4 - 1))
        p.wait(g, "gpsimd", "p1_kv_0_" + str(TB4 - 1))
        p.wait(g, "gpsimd", f"p1_vtr_0_{TB4-1}_ev")
        for idx, srcb in enumerate([qh, kh, vtm]):
            ci = g.dma_start(out=dump_d.rearrange("(c p) n -> p c n", p=128)[:, idx:idx+1, :],
                             in_=srcb.unsqueeze(1) if hasattr(srcb, 'unsqueeze') else srcb)
            p.inc(ci, "dO", 16)
    if dump == "prh0":
        p.wait(g, "gpsimd", "p1_exp_0_0")
        p.wait(g, "gpsimd", "p1_exp_0_1")
        for idx, srcb in enumerate(prb):
            ci = g.dma_start(out=dump_d.rearrange("(c p) n -> p c n", p=128)[:, idx:idx+1, :],
                             in_=srcb.unsqueeze(1) if hasattr(srcb, 'unsqueeze') else srcb)
            p.inc(ci, "dO", 16)

    if dump == "zt":
        p.wait(g, "gpsimd", "attn_done")
        ci = g.dma_start(out=dump_d.rearrange("(c p) n -> p c n", p=128),
                          in_=zy[:, 0:H * S].rearrange("p (c n) -> p c n", n=S))
        p.inc(ci, "dO", 16)

    if cfg.upto in ("p1", "p1h0"):
        if not p.dry:
            sy.wait_ge(p.sem["dO"], p.cnt["dO"])
        return
    for hc in range(H):
        p.wait(g, "gpsimd", f"p1_qkvmm_{H-1}")
        ci = g.dma_start(out=fm[:, hc * D:(hc + 1) * D], in_=wo[hc])
        p.inc(ci, "dW", 16)
    p.mark("p2_wo", "dW")

    if cfg.upto == "p2w":
        if dump == "wofm":
            p.wait(g, "gpsimd", "p2_wo")
            ci = g.dma_start(out=dump_d.rearrange("(c p) n -> p c n", p=128),
                             in_=fm[:, 0:H * D].rearrange("p (c n) -> p c n", n=D))
            p.inc(ci, "dO", 16)
        if not p.dry:
            sy.wait_ge(p.sem["dO"], p.cnt["dO"])
        return

    def p2_load_x0(tb):
        if tb >= TB:
            return
        if tb < 2:
            p.wait(sy, "sync", f"p0_ln_{TB-1}")
            p.wait(sy, "sync", f"p1_z_{H-1}_{QG-1}")
        else:
            p.wait(sy, "sync", f"p2_x1st_{tb-2}")
        ci = sy.dma_start(out=x0b[tb % 2], in_=x_t[tb, :, :])
        p.inc(ci, "dX", 16)
        p.mark(f"p2_x0_{tb}", "dX")

    if cfg.upto not in ("p2m", "p2t"):
        p2_load_x0(0)
        p2_load_x0(1)

    for tb in range(TB):
        par = tb % 2
        for db in range(HB):
            pa = pb0[:, (db % 2) * 512:(db % 2) * 512 + 512]
            k = tb * HB + db
            if k == 0:
                p.wait(t, "tensor", "attn_done")
                p.wait(t, "tensor", "p2_wo")
            gap = 2 if HB >= 2 else 1
            if k >= gap:
                pk = k - gap
                p.wait(t, "tensor", f"p2_aev_{pk // HB}_{pk % HB}")
                p.wait(t, "tensor", f"p2_st_{pk // HB}_{pk % HB}")
            ci = None
            for hc in range(H):
                ci = t.matmul(pa, zy[:, hc * S + tb * 128: hc * S + (tb + 1) * 128],
                              fm[:, hc * D + db * 512: hc * D + (db + 1) * 512],
                              start=(hc == 0), stop=(hc == H - 1))
            p.inc(ci, "tP")
            p.mark(f"p2_attn_{tb}_{db}", "tP")
            if cfg.upto == "p2m":
                p.mark(f"p2_st_{tb}_{db}", "tS")
            else:
                p.wait(v, "vector", f"p2_aev_{tb}_{db}")
                if db == 0 and tb >= 2 and cfg.upto != "p2a":
                    p.wait(v, "vector", f"p2_negmr_{tb-2}")
                ci = v.bn_stats(bnb[:, par * 32 + 6 * db: par * 32 + 6 * db + 6],
                                araw[:, db * 512:(db + 1) * 512])
                p.inc(ci, "tV")
                p.mark(f"p2_st_{tb}_{db}", "tV")
            p.wait(s, "scalar", f"p2_attn_{tb}_{db}")
            if tb >= 2 and db == 0:
                p.wait(s, "scalar", f"p2_x1g_{tb-2}")
            ci = s.copy(araw[:, db * 512:(db + 1) * 512], pa)
            p.inc(ci, "tS")
            p.mark(f"p2_aev_{tb}_{db}", "tS")
        if cfg.upto in ("p2a", "p2m"):
            p.mark(f"p2_x1g_{tb}", "tS")
            p.mark(f"p2_x1_{tb}", "tS")
            p.mark(f"p2_x1st_{tb}", "tS")
            continue
        v.drain()
        ci = v.bn_aggr(mvb[:, par * 2: par * 2 + 2],
                       bnb[:, par * 32: par * 32 + 6 * HB])
        p.inc(ci, "tV")
        p.mark(f"p2_mv_{tb}", "tV")
        p.wait(s, "scalar", f"p2_mv_{tb}")
        s.activation(tln[:, par: par + 1], mvb[:, par * 2 + 1: par * 2 + 2],
                     AF.Ln, bias=epsb[:, 0:1], scale=1.0)
        s.drain()
        ci = s.activation(rstd[:, par: par + 1], tln[:, par: par + 1],
                          AF.Exp, bias=0.0, scale=-0.5)
        p.inc(ci, "tS")
        p.mark(f"p2_rstd_{tb}", "tS")
        s.drain()
        p.wait(v, "vector", f"p2_rstd_{tb}")
        ci = v.tensor_scalar(negmr[:, par: par + 1], mvb[:, par * 2: par * 2 + 1],
                             rstd[:, par: par + 1], -1.0, OP.mult, OP.mult)
        p.inc(ci, "tV")
        p.mark(f"p2_negmr_{tb}", "tV")
        p.wait(s, "scalar", f"p2_negmr_{tb}")
        if tb >= 1:
            p.wait(s, "scalar", f"p2_x1g_{tb-1}")
        ci = s.activation(t1b, araw, AF.Identity,
                          bias=negmr[:, par: par + 1], scale=rstd[:, par: par + 1])
        p.inc(ci, "tS")
        p.mark(f"p2_t1_{tb}", "tS")
        if cfg.upto in ("p2s", "p2t"):
            p.mark(f"p2_x1_{tb}", "tS")
            p.mark(f"p2_x1g_{tb}", "tS")
            p.mark(f"p2_x1st_{tb}", "tS")
            continue
        p.wait(g, "gpsimd", f"p2_t1_{tb}")
        p.wait(g, "gpsimd", f"p2_x0_{tb}")
        ci = g.tensor_tensor(x0b[par], t1b, x0b[par], OP.add)
        p.inc(ci, "tG")
        p.mark(f"p2_x1_{tb}", "tG")
        p.mark(f"p2_x1g_{tb}", "tG")

        p.wait(sy, "sync", f"p2_x1_{tb}")
        ci = sy.dma_start(out=x1_t[tb, :, :], in_=x0b[par])
        p.inc(ci, "dX", 16)
        p.mark(f"p2_x1st_{tb}", "dX")
        p2_load_x0(tb + 2)
    p.mark("p2_done", "tG" if cfg.upto not in ("p2a", "p2m", "p2s", "p2t") else "tS")
    if cfg.upto in ("p2a", "p2m", "p2s", "p2t"):
        if dump == "attnraw":
            p.wait(g, "gpsimd", f"p2_aev_{TB-1}_{HB-1}")
            ci = g.dma_start(out=dump_d, in_=araw)
            p.inc(ci, "dO", 16)
        if not p.dry:
            sy.wait_ge(p.sem["dO"], p.cnt["dO"])
        return

    if dump == "x1":
        p.wait(sy, "sync", "p2_done")
        for tb in range(TB):
            p.wait(sy, "sync", f"p2_x1st_{tb}")
        ci = sy.dma_start(out=dump_d.rearrange("(t p) d -> t p d", p=128), in_=x1_t)
        p.inc(ci, "dO", 16)

    if cfg.upto == "p2":
        if not p.dry:
            sy.wait_ge(p.sem["dO"], p.cnt["dO"])
        return
    for tb in range(TB):
        par = tb % 2
        if tb < 2:
            p.wait(sy, "sync", f"p1_qkvmm_{H-1}")
        else:
            p.wait(sy, "sync", f"p2b_ln2_{tb-2}")
        p.wait(sy, "sync", f"p2_x1st_{tb}")
        ci = sy.dma_start(out=x1b[par], in_=x1_t[tb, :, :])
        p.inc(ci, "dX", 16)
        p.mark(f"p2b_x1_{tb}", "dX")
        p.wait(v, "vector", f"p2b_x1_{tb}")
        if tb >= 2:
            p.wait(v, "vector", f"p2b_rstd_{tb-2}")
        else:
            p.wait(v, "vector", f"p2_negmr_{TB-1}")
        for j in range(D // 512):
            v.bn_stats(bnb[:, par * 32 + 6 * j: par * 32 + 6 * j + 6],
                       x1b[par][:, j * 512:(j + 1) * 512])
        v.drain()
        ci = v.bn_aggr(mvb[:, par * 2: par * 2 + 2],
                       bnb[:, par * 32: par * 32 + 6 * (D // 512)])
        p.inc(ci, "tV")
        p.mark(f"p2b_mv_{tb}", "tV")
        p.wait(s, "scalar", f"p2b_mv_{tb}")
        if tb < 2:
            p.wait(s, "scalar", f"p2_t1_{TB-1}")
        s.activation(tln[:, par: par + 1], mvb[:, par * 2 + 1: par * 2 + 2],
                     AF.Ln, bias=epsb[:, 0:1], scale=1.0)
        s.drain()
        ci = s.activation(rstd[:, par: par + 1], tln[:, par: par + 1],
                          AF.Exp, bias=0.0, scale=-0.5)
        p.inc(ci, "tS")
        p.mark(f"p2b_rstd_{tb}", "tS")
        s.drain()
        p.wait(v, "vector", f"p2b_rstd_{tb}")
        ci = v.tensor_scalar(negmr[:, par: par + 1], mvb[:, par * 2: par * 2 + 1],
                             rstd[:, par: par + 1], -1.0, OP.mult, OP.mult)
        p.inc(ci, "tV")
        p.mark(f"p2b_negmr_{tb}", "tV")
        p.wait(s, "scalar", f"p2b_negmr_{tb}")
        if tb >= 1:
            p.wait(s, "scalar", f"p2b_tr_{tb-1}_{DC//4-1}")
        if tb < 2:
            p.wait(s, "scalar", f"p1_qkvmm_{H-1}")
        ci = s.activation(ln2f, x1b[par], AF.Identity,
                          bias=negmr[:, par: par + 1], scale=rstd[:, par: par + 1])
        p.inc(ci, "tS")
        p.mark(f"p2b_ln2_{tb}", "tS")
        p.wait(t, "tensor", f"p2b_ln2_{tb}")
        if tb == 0:
            p.wait(t, "tensor", f"p2_aev_{TB-1}_{HB-1}")
        for g4 in range(DC // 4):
            k = tb * (DC // 4) + g4
            bpar = k % 2
            bank = pb2 if bpar == 0 else pb1
            if k >= 2:
                pk = k - 2
                p.wait(t, "tensor", f"p2b_rotev_{pk//(DC//4)}_{pk%(DC//4)}")
                p.wait(t, "tensor", f"p2b_cast_{pk//(DC//4)}_{pk%(DC//4)}")
            ci = None
            for j in range(4):
                dc = g4 * 4 + j
                ci = t.transpose(bank[:, j * 128:(j + 1) * 128],
                                 ln2f[:, dc * 128:(dc + 1) * 128], id_f[:])
            p.inc(ci, "tP")
            p.mark(f"p2b_tr_{tb}_{g4}", "tP")
            p.wait(v, "vector", f"p2b_tr_{tb}_{g4}")
            if k >= 2:
                p.wait(v, "vector", f"p2b_gate_{(k-2)//(DC//4)}_{(k-2)%(DC//4)}")
                p.wait(v, "vector", f"p2b_cast_{(k-2)//(DC//4)}_{(k-2)%(DC//4)}")
            ci = v.tensor_copy(rot[bpar], bank[:])
            p.inc(ci, "tV")
            p.mark(f"p2b_rotev_{tb}_{g4}", "tV")
            p.wait(g, "gpsimd", f"p2b_rotev_{tb}_{g4}")
            if tb == 0 and g4 == 0:
                p.wait(g, "gpsimd", "p2_done")
                p.wait(g, "gpsimd", f"p2_attn_{TB-1}_{HB-1}")
            dst = bass.AP(fm, (g4 * 4) * S + tb * 128,
                          [[fm.shape[1], 128], [S, 4], [1, 128]])
            src = rot[bpar].rearrange("p (c n) -> p c n", n=128)
            ci = g.tensor_copy(dst, src)
            p.inc(ci, "tG")
            p.mark(f"p2b_cast_{tb}_{g4}", "tG")
            p.wait(t, "tensor", f"p2b_rotev_{tb}_{g4}")
            p.wait(t, "tensor", "gw")
            if g4 == 0:
                if tb > 0:
                    p.wait(t, "tensor", f"p2b_pexp_{tb-1}")
                else:
                    p.wait(t, "tensor", f"p1_bcev_{H-1}_{QG-1}")
            ci = None
            for j in range(4):
                dc = g4 * 4 + j
                ci = t.matmul(pb4[:, 0:E], rot[bpar][:, j * 128:(j + 1) * 128],
                              gw_sb[:, dc * E:(dc + 1) * E],
                              start=(dc == 0), stop=(dc == DC - 1))
            p.inc(ci, "tP")
            p.mark(f"p2b_gate_{tb}_{g4}", "tP")
        p.mark(f"p2b_fm_{tb}", "tG")
        p.wait(v, "vector", f"p2b_gate_{tb}_{DC//4-1}")
        ci = v.reduce_max(m8[:, 0:1], pb4[:, 0:E], axis=AX.X, negate=True)
        p.inc(ci, "tV")
        p.mark(f"p2b_lmax_{tb}", "tV")
        p.wait(s, "scalar", f"p2b_lmax_{tb}")
        ci = s.activation(pr8[:], pb4[:, 0:E], AF.Exp, bias=m8[:, 0:1],
                          scale=1.0, accum_out=l8[:, 0:1])
        p.inc(ci, "tS")
        p.mark(f"p2b_pexp_{tb}", "tS")
        p.wait(v, "vector", f"p2b_pexp_{tb}")
        v.reciprocal(r8[:], l8[:])
        v.drain()
        v.reduce_max(m8[:, 1:2], pr8[:], axis=AX.X)
        v.drain()
        v.tensor_scalar(mk1[:], pr8[:], m8[:, 1:2], None, OP.is_ge)
        v.drain()
        v.scalar_tensor_tensor(s2b[:], mk1[:], -2.0, pr8[:], OP.mult, OP.add)
        v.drain()
        v.reduce_max(m8[:, 1:2], s2b[:], axis=AX.X)
        v.drain()
        v.tensor_scalar(mk2[:], s2b[:], m8[:, 1:2], None, OP.is_ge)
        v.drain()
        v.tensor_tensor(mk1[:], mk1[:], mk2[:], OP.add)
        v.drain()
        ci = v.scalar_tensor_tensor(comb[:, tb * E:(tb + 1) * E], mk1[:],
                                    r8[:, 0:1], pr8[:], OP.mult, OP.mult)
        p.inc(ci, "tV")
        p.mark(f"p2b_comb_{tb}", "tV")
    p.mark("p2b_done", "tG")
    if p.dry:
        for tb4 in range(TB4):
            tt = min(tb4 * 4 + 3, TB - 1)
            p.marks[f"p2b_fm_ready_{tb4}"] = p.marks[f"p2b_cast_{tt}_{DC//4-1}"]

    if dump == "ln2fm":
        p.wait(g, "gpsimd", f"p2b_fm_{TB-1}")
        p.wait(g, "gpsimd", f"p2b_cast_{TB-1}_{DC//4-1}")
        ci = g.dma_start(out=dump_d.rearrange("(c p) n -> p c n", p=128),
                          in_=fm[:, 0:DC * S].rearrange("p (c n) -> p c n", n=S))
        p.inc(ci, "dO", 16)
    if dump == "comb":
        p.wait(sy, "sync", f"p2b_comb_{TB-1}")
        ci = sy.dma_start(out=dump_d.rearrange("(t p) n -> p t n", p=128),
                          in_=comb[:, 0:TB * E].rearrange("p (t n) -> p t n", n=E))
        p.inc(ci, "dO", 16)

    if cfg.upto == "p2b":
        if not p.dry:
            sy.wait_ge(p.sem["dO"], p.cnt["dO"])
        return
    for ex in range(NEX):
        for fc in range(FC):
            k = ex * FC + fc
            if k < 2:
                p.wait(g, "gpsimd", f"p2b_ln2_{TB-1}")
                p.wait(g, "gpsimd", f"p2b_gate_{TB-1}_{DC//4-1}")
            else:
                p.wait(g, "gpsimd", f"p3_gu_used_{(k-2)//FC}_{(k-2)%FC}")
            ci = g.dma_start(out=gub[k % 2].rearrange("p (c n) -> p c n", n=256),
                             in_=gu_w[ex, fc].rearrange("c p n -> p c n"))
            p.inc(ci, "dW", 16)
            p.mark(f"p3_guw_{ex}_{fc}", "dW")
        for hb2 in range(HB2):
            k = ex * HB2 + hb2
            if k < 2:
                p.wait(g, "gpsimd", f"p2b_ln2_{TB-1}")
                p.wait(g, "gpsimd", f"p2b_gate_{TB-1}_{DC//4-1}")
            else:
                p.wait(g, "gpsimd", f"p3_dn_used_{(k-2)//HB2}_{(k-2)%HB2}")
            src = bass.AP(dn_w.tensor, (ex * FC * 128) * D + hb2 * 256,
                          [[D, 128], [128 * D, FC], [1, 256]])
            ci = g.dma_start(out=dnb[k % 2].rearrange("p (c n) -> p c n", n=256),
                             in_=src)
            p.inc(ci, "dW", 16)
            p.mark(f"p3_dnw_{ex}_{hb2}", "dW")

    ytile = 0
    for ex in range(NEX):
        for fc in range(FC):
            p.wait(t, "tensor", f"p3_guw_{ex}_{fc}")
            if ex == 0 and fc == 0:
                p.wait(t, "tensor", f"p2b_gate_{TB-1}_{DC//4-1}")
                p.wait(t, "tensor", f"p2_aev_{TB-1}_{HB-1}")
            gb = gub[(ex * FC + fc) % 2]
            for tb in range(TB4):
                kk = ex * FC * TB4 + fc * TB4 + tb
                gpar = kk % 2
                if kk >= 2:
                    pk = kk - 2
                    p.wait(t, "tensor",
                           f"p3_act_{pk // (FC*TB4)}_{(pk % (FC*TB4)) // TB4}_{pk % TB4}")
                if ex == 0 and fc == 0 and tb < 2:
                    p.wait(t, "tensor", f"p2b_fm_ready_{tb}")
                for which in range(2):
                    dst = pb0[:, gpar * 1024 + which * 512: gpar * 1024 + (which + 1) * 512]
                    ci = None
                    for dc in range(DC):
                        ci = t.matmul(dst,
                                      gb[:, dc * 256 + which * 128: dc * 256 + which * 128 + 128],
                                      fm[:, dc * S + tb * 512: dc * S + tb * 512 + 512],
                                      start=(dc == 0), stop=(dc == DC - 1))
                    p.inc(ci, "tP")
                p.mark(f"p3_gu_{ex}_{fc}_{tb}", "tP")
                p.wait(s, "scalar", f"p3_gu_{ex}_{fc}_{tb}")
                if kk >= 2:
                    pk = kk - 2
                    p.wait(s, "scalar",
                           f"p3_act_{pk // (FC*TB4)}_{(pk % (FC*TB4)) // TB4}_{pk % TB4}")
                ci = s.activation(s1b[:, gpar * 512:(gpar + 1) * 512],
                                  pb0[:, gpar * 1024: gpar * 1024 + 512],
                                  AF.Silu, bias=0.0, scale=1.0)
                p.inc(ci, "tS")
                p.mark(f"p3_silu_{ex}_{fc}_{tb}", "tS")
                p.wait(v, "vector", f"p3_silu_{ex}_{fc}_{tb}")
                if ex == 0 and fc * TB4 + tb < 2:
                    p.wait(v, "vector", "p2b_done")
                    p.wait(v, "vector", f"p2_x1g_{TB-1}")
                ci = v.tensor_tensor(act[:, fc * S + tb * 512: fc * S + tb * 512 + 512],
                                     pb0[:, gpar * 1024 + 512: gpar * 1024 + 1024],
                                     s1b[:, gpar * 512:(gpar + 1) * 512], OP.mult)
                p.inc(ci, "tV")
                p.mark(f"p3_act_{ex}_{fc}_{tb}", "tV")
            p.mark(f"p3_gu_used_{ex}_{fc}", "tP")
        pyb = [pb1, pb2, pb3, pb4]
        for hb2 in range(HB2):
            p.wait(t, "tensor", f"p3_dnw_{ex}_{hb2}")
            if ex == 0 and hb2 == 0:
                p.wait(t, "tensor", f"p1_rtr_{H-1}_{TB-1}")
                p.wait(t, "tensor", f"p2b_gate_{TB-1}_{DC//4-1}")
            db = dnb[(ex * HB2 + hb2) % 2]
            for tb in range(TB):
                kk = ytile
                if kk >= 4:
                    pk = kk - 4
                    p.wait(t, "tensor",
                           f"p3_yev_{pk // (HB2*TB)}_{(pk % (HB2*TB)) // TB}_{pk % TB}")
                if hb2 == 0 and tb == 0:
                    p.wait(t, "tensor", f"p3_act_{ex}_{FC-1}_{TB4-1}")
                ci = None
                for fc in range(FC):
                    ci = t.matmul(pyb[kk % 4][:, 0:256],
                                  act[:, fc * S + tb * 128: fc * S + (tb + 1) * 128],
                                  db[:, fc * 256:(fc + 1) * 256],
                                  start=(fc == 0), stop=(fc == FC - 1))
                p.inc(ci, "tP")
                p.mark(f"p3_y_{ex}_{hb2}_{tb}", "tP")
                p.wait(v, "vector", f"p3_y_{ex}_{hb2}_{tb}")
                ydst = zy[:, tb * D + hb2 * 256: tb * D + (hb2 + 1) * 256]
                if ex == 0:
                    ci = v.tensor_copy(ydst, pyb[kk % 4][:, 0:256])
                elif ex == 1:
                    ci = v.tensor_tensor(ydst, pyb[kk % 4][:, 0:256], ydst, OP.add)
                else:
                    p.wait(v, "vector", f"p2b_comb_{tb}")
                    ci = v.scalar_tensor_tensor(
                        ydst, pyb[kk % 4][:, 0:256],
                        comb[:, tb * E + (ex - 2): tb * E + (ex - 2) + 1],
                        ydst, OP.mult, OP.add)
                p.inc(ci, "tV")
                p.mark(f"p3_yev_{ex}_{hb2}_{tb}", "tV")
                ytile += 1
            p.mark(f"p3_dn_used_{ex}_{hb2}", "tP")
    p.mark("p3_done", "tV")

    if dump == "act":
        p.wait(g, "gpsimd", f"p3_act_{NEX-1}_{FC-1}_{TB4-1}")
        ci = g.dma_start(out=dump_d.rearrange("(c p) n -> p c n", p=128),
                          in_=act[:, 0:FC * S].rearrange("p (c n) -> p c n", n=S))
        p.inc(ci, "dO", 16)
    if dump == "y":
        p.wait(g, "gpsimd", "p3_done")
        ci = g.dma_start(out=dump_d.rearrange("(c p) n -> p c n", p=128),
                          in_=zy[:, 0:TB * D].rearrange("p (c n) -> p c n", n=D))
        p.inc(ci, "dO", 16)

    if cfg.upto == "p3":
        if not p.dry:
            sy.wait_ge(p.sem["dO"], p.cnt["dO"])
        return
    t2b = wkb[:, 0:D]
    ob = [wkb[:, D + i * D: D + (i + 1) * D] for i in range(2)]
    x1r = [wkb[:, 3 * D + i * D: 3 * D + (i + 1) * D] for i in range(2)]

    for tb in range(TB):
        par = tb % 2
        if tb < 2:
            p.wait(sy, "sync", "p3_done")
        else:
            p.wait(sy, "sync", f"p4_out_{tb-2}")
        ci = sy.dma_start(out=x1r[par], in_=x1_t[tb, :, :])
        p.inc(ci, "dX", 16)
        p.mark(f"p4_x1_{tb}", "dX")
        p.wait(v, "vector", f"p3_yev_{NEX-1}_{HB2-1}_{tb}")
        if tb < 2:
            p.wait(v, "vector", f"p2b_comb_{TB-1}")
        if tb >= 2:
            p.wait(v, "vector", f"p4_rstd_{tb-2}")
        yv = zy[:, tb * D:(tb + 1) * D]
        for j in range(D // 512):
            v.bn_stats(bnb[:, par * 32 + 6 * j: par * 32 + 6 * j + 6],
                       yv[:, j * 512:(j + 1) * 512])
        v.drain()
        ci = v.bn_aggr(mvb[:, par * 2: par * 2 + 2],
                       bnb[:, par * 32: par * 32 + 6 * (D // 512)])
        p.inc(ci, "tV")
        p.mark(f"p4_mv_{tb}", "tV")
        p.wait(s, "scalar", f"p4_mv_{tb}")
        s.activation(tln[:, par: par + 1], mvb[:, par * 2 + 1: par * 2 + 2],
                     AF.Ln, bias=epsb[:, 0:1], scale=1.0)
        s.drain()
        ci = s.activation(rstd[:, par: par + 1], tln[:, par: par + 1],
                          AF.Exp, bias=0.0, scale=-0.5)
        p.inc(ci, "tS")
        p.mark(f"p4_rstd_{tb}", "tS")
        s.drain()
        p.wait(v, "vector", f"p4_rstd_{tb}")
        ci = v.tensor_scalar(negmr[:, par: par + 1], mvb[:, par * 2: par * 2 + 1],
                             rstd[:, par: par + 1], -1.0, OP.mult, OP.mult)
        p.inc(ci, "tV")
        p.mark(f"p4_negmr_{tb}", "tV")
        p.wait(s, "scalar", f"p4_negmr_{tb}")
        if tb < 2:
            p.wait(s, "scalar", f"p3_y_{NEX-1}_{HB2-1}_{TB-1}")
        if tb >= 1:
            p.wait(s, "scalar", f"p4_out_{tb-1}")
        ci = s.activation(t2b, yv, AF.Identity,
                          bias=negmr[:, par: par + 1], scale=rstd[:, par: par + 1])
        p.inc(ci, "tS")
        p.mark(f"p4_t2_{tb}", "tS")
        p.wait(v, "vector", f"p4_t2_{tb}")
        p.wait(v, "vector", f"p4_x1_{tb}")
        if tb >= 2:
            p.wait(v, "vector", f"p4_outst_{tb-2}")
        ci = v.tensor_tensor(ob[par], t2b, x1r[par], OP.add)
        p.inc(ci, "tV")
        p.mark(f"p4_out_{tb}", "tV")
        p.wait(sy, "sync", f"p4_out_{tb}")
        ci = sy.dma_start(out=out_t[tb, :, :], in_=ob[par])
        p.inc(ci, "dO", 16)
        p.mark(f"p4_outst_{tb}", "dO")

    if not p.dry:
        sy.wait_ge(p.sem["dO"], p.cnt["dO"])



def build_program(cfg: Cfg):
    S, D, H, E, F = cfg.S, cfg.D, cfg.H, cfg.E, cfg.F
    DC, TB, FC, NEX = cfg.DC, cfg.TB, cfg.FC, cfg.NEX
    nc = bass.Bass("TRN2", target_bir_lowering=False)

    x_in = nc.dram_tensor("x", [S, D], F32, kind="ExternalInput").ap()
    wqkv = nc.dram_tensor("wqkv", [H, DC, 128, 384], F32, kind="ExternalInput").ap()
    wo = nc.dram_tensor("wo", [H, 128, D], F32, kind="ExternalInput").ap()
    gatew = nc.dram_tensor("gatew", [DC, 128, E], F32, kind="ExternalInput").ap()
    gu_w = nc.dram_tensor("gu_w", [NEX, FC, DC, 128, 256], F32, kind="ExternalInput").ap()
    dn_w = nc.dram_tensor("dn_w", [NEX, FC, 128, D], F32, kind="ExternalInput").ap()
    out_d = nc.dram_tensor("out", [S, D], F32, kind="ExternalOutput").ap()
    x1_d = nc.dram_tensor("x1scr", [S, D], F32, kind="Internal").ap()
    dump_d = None
    if cfg.dump is not None:
        shp = {"ln1fm": [DC * 128, S], "zt": [H * 128, S], "x1": [S, D],
               "ln2fm": [DC * 128, S], "comb": [S, E], "act": [FC * 128, S],
               "y": [S, D], "qkvh0": [3 * 128, S], "prh0": [2 * 128, S],
               "zth0": [128, S], "prtend": [128, 2 * S],
               "bcend": [128, 256], "attnraw": [128, D],
               "wofm": [None, D]}[cfg.dump]
        if cfg.dump == "wofm":
            shp = [cfg.H * 128, D]
        dump_d = nc.dram_tensor("dump", shp, F32, kind="ExternalOutput").ap()

    x_t = x_in.rearrange("(t p) d -> t p d", p=128)
    x1_t = x1_d.rearrange("(t p) d -> t p d", p=128)
    out_t = out_d.rearrange("(t p) d -> t p d", p=128)
    io = (x_t, x1_t, out_t, wqkv, wo, gatew, gu_w, dn_w, dump_d)

    with ExitStack() as st:
        p = P(nc, st)
        sb = lambda n, sh, dt: st.enter_context(nc.sbuf_tensor(n, sh, dt))
        fm = sb("fm", [128, max(DC, H) * S], BF16)
        zy = sb("zy", [128, max(H, TB * D // S) * S], BF16)
        wk = sb("wk", [128, max(7 * S + 2048, FC * S, 10 * D)], BF16)
        wg = sb("wg", [128, max(2 * DC * 384, 2 * DC * 256 + 2 * FC * 256, 6 * D + 2048)], BF16)
        comb = sb("comb", [128, TB * E], F32)
        gw_sb = sb("gw_sb", [128, DC * E], F32)
        ones = sb("ones", [128, 128], F32)
        id_f = sb("id_f", [128, 128], F32)
        id_b = sb("id_b", [128, 128], BF16)
        mask_b = sb("mask_b", [128, 128], BF16)
        bnb = sb("bnb", [128, 64], F32)
        mvb = sb("mvb", [128, 4], F32)
        tln = sb("tln", [128, 2], F32)
        rstd = sb("rstd", [128, 2], F32)
        negmr = sb("negmr", [128, 2], F32)
        lsum = sb("lsum", [128, 2], F32)
        rcol = sb("rcol", [128, 2], F32)
        nmax = sb("nmax", [128, 2], F32)
        rrow = sb("rrow", [1, 256], F32)
        s1b = sb("s1b", [128, 1024], BF16)
        pr8 = sb("pr8", [128, E], F32)
        s2b = sb("s2b", [128, E], F32)
        mk1 = sb("mk1", [128, E], F32)
        mk2 = sb("mk2", [128, E], F32)
        l8 = sb("l8", [128, 1], F32)
        r8 = sb("r8", [128, 1], F32)
        m8 = sb("m8", [128, 2], F32)
        epsb = sb("epsb", [128, 1], F32)
        pb0 = st.enter_context(nc.psum_tensor("pb0", [128, 2048], F32))
        pb1 = st.enter_context(nc.psum_tensor("pb1", [128, 512], F32))
        pb2 = st.enter_context(nc.psum_tensor("pb2", [128, 512], F32))
        pb3 = st.enter_context(nc.psum_tensor("pb3", [128, 512], F32))
        pb4 = st.enter_context(nc.psum_tensor("pb4", [128, 512], F32))
        buf = (fm, zy, wk, wg, comb, gw_sb, ones, id_f, id_b, mask_b,
               bnb, mvb, tln, rstd, negmr, lsum, rcol, nmax, rrow, s1b,
               pr8, s2b, mk1, mk2, l8, r8, m8, epsb,
               pb0, pb1, pb2, pb3, pb4)
        block = st.enter_context(nc.Block())

        fake = {n: FakeEng() for n in ("sync", "gpsimd", "tensor", "vector", "scalar")}
        p.reset(dry=True)
        emit(p, cfg, fake, io, buf)
        for name in ("sync", "gpsimd", "tensor", "vector", "scalar"):
            sec = getattr(block, name)

            def body(engobj, name=name):
                eng = dict(fake)
                eng[name] = engobj
                p.reset(dry=False)
                emit(p, cfg, eng, io, buf)

            sec(body)
    return nc


def _layout(WK, WQ, WV, WO, GW, GUW, DNW, SGU, SDN, cfg: Cfg):
    H, D, E, F, FC, DC = cfg.H, cfg.D, cfg.E, cfg.F, cfg.FC, cfg.DC
    DH, FS = 128, cfg.FS
    f32 = np.float32
    tr = lambda w: np.ascontiguousarray(np.transpose(np.asarray(w, f32), (0, 2, 1)))
    wqt = tr(WQ).reshape(H, DC, 128, DH)
    wkt = tr(WK).reshape(H, DC, 128, DH)
    wvt = tr(WV).reshape(H, DC, 128, DH)
    wqkv = np.ascontiguousarray(np.concatenate([wqt, wkt, wvt], axis=-1))
    wo = np.ascontiguousarray(np.asarray(WO, f32).T.reshape(H, 128, D))
    gatew = np.ascontiguousarray(np.asarray(GW, f32).T.reshape(DC, 128, E))

    def gu_slabs(gT):
        nf = gT.shape[-1] // 2
        fc = nf // 128
        gg = gT[..., :nf].reshape(-1, DC, 128, fc, 128).transpose(0, 3, 1, 2, 4)
        uu = gT[..., nf:].reshape(-1, DC, 128, fc, 128).transpose(0, 3, 1, 2, 4)
        return np.concatenate([gg, uu], axis=-1)

    guT = tr(GUW)
    gue = gu_slabs(guT)
    sgT = np.asarray(SGU, f32).T
    sa = np.concatenate([sgT[:, 0:F], sgT[:, FS:FS + F]], axis=1)
    sb_ = np.concatenate([sgT[:, F:FS], sgT[:, FS + F:2 * FS]], axis=1)
    gus = gu_slabs(np.stack([sa, sb_]))
    gu_all = np.ascontiguousarray(np.concatenate([gus, gue], axis=0))

    dne = tr(DNW).reshape(E, FC, 128, D)
    dns = np.ascontiguousarray(np.asarray(SDN, f32).T).reshape(2, FC, 128, D)
    dn_all = np.ascontiguousarray(np.concatenate([dns, dne], axis=0))
    return {"wqkv": wqkv, "wo": wo, "gatew": gatew, "gu_w": gu_all, "dn_w": dn_all}


def kernel(L0_attn_x, L0_attn_W_K, L0_attn_W_Q, L0_attn_W_V, L0_attn_W_O,
           L0_ffn_gate_weight, L0_ffn_gate_up_weights, L0_ffn_down_weights,
           L0_ffn_shared_gate_up, L0_ffn_shared_down):
    cfg = Cfg()
    w = _layout(L0_attn_W_K, L0_attn_W_Q, L0_attn_W_V, L0_attn_W_O,
                L0_ffn_gate_weight, L0_ffn_gate_up_weights, L0_ffn_down_weights,
                L0_ffn_shared_gate_up, L0_ffn_shared_down, cfg)
    x = np.asarray(L0_attn_x, np.float32)
    nc = build_program(cfg)
    in_maps = []
    for b in range(x.shape[0]):
        m = dict(w)
        m["x"] = np.ascontiguousarray(x[b])
        in_maps.append(m)
    res = run_bass_kernel_spmd(nc, in_maps, core_ids=list(range(len(in_maps))))
    return np.stack([np.asarray(r["out"]) for r in res.results], axis=0)
